# version 16
# baseline (speedup 1.0000x reference)
from contextlib import ExitStack

import numpy as np
import ml_dtypes

import concourse.bass as bass
import concourse.mybir as mybir
from concourse.bass_utils import run_bass_kernel_spmd

F32 = mybir.dt.float32
BF16 = mybir.dt.bfloat16
AF = mybir.ActivationFunctionType
ALU = mybir.AluOpType
AX = mybir.AxisListType
BF = ml_dtypes.bfloat16

S, D, H, HD = 2048, 1024, 16, 64
NT, NC = S // 128, D // 128
FF, E, HID = 2048, 32, 512
EPS = 1e-5
GELU_C, GELU_S = 0.044715, 0.7978845608028654
NEGBIG = -2.0e32
KB = 1024

ST_MU, ST_RSTD, ST_NEG, ST_RP, ST_T0, ST_T1 = 0, 16, 32, 48, 64, 80


class Ch:
    def __init__(self, nc, stack, name):
        self.sem = stack.enter_context(nc.semaphore(name))
        self.n = 0

    def inc(self, instr, k=1):
        instr.then_inc(self.sem, k)
        self.n += k
        return self.n


class P:
    def __init__(self):
        nc = bass.Bass("TRN2", target_bir_lowering=False, debug=False)
        self.nc = nc
        self.stack = ExitStack()
        self.v = Ch(nc, self.stack, "chv")
        self.s = Ch(nc, self.stack, "chs")
        self.t = Ch(nc, self.stack, "cht")
        self.g = Ch(nc, self.stack, "chg")
        self.w = [Ch(nc, self.stack, f"chw{i}") for i in range(4)]
        self.xs = Ch(nc, self.stack, "chxs")
        self.ga = [Ch(nc, self.stack, f"chga{i}") for i in range(2)]
        self.si = [Ch(nc, self.stack, f"chsi{i}") for i in range(2)]
        self.ys = Ch(nc, self.stack, "chys")
        self.cv = Ch(nc, self.stack, "chcv")
        self.d = Ch(nc, self.stack, "chd")
        self.gp = Ch(nc, self.stack, "chgp")
        self._lw = {}

    def wait(self, eng, ch, val=None):
        if val is None:
            val = ch.n
        if val <= 0:
            return
        key = (id(eng), id(ch))
        if self._lw.get(key, -1) >= val:
            return
        eng.wait_ge(ch.sem, val)
        self._lw[key] = val

    SB_BASE = 16544

    def sb(self, name, off, shape, dtype):
        return self.nc.alloc_sbuf_tensor_at(name, shape, dtype,
                                            offset=self.SB_BASE + off)


def _consts_np():
    c = {}
    c["ident"] = np.eye(128, dtype=BF)
    c["ones_row"] = np.ones((1, 128), dtype=BF)
    c["ones_row_f32"] = np.ones((1, 128), dtype=np.float32)
    cm = np.zeros((128, 1), np.float32)
    cm[127, 0] = -999.0 * 32.0
    c["colmask"] = cm
    m = np.zeros((128, 4 * 512), np.float32)
    for r in range(4):
        kk = np.arange(128)[:, None]
        qq = np.arange(512)[None, :]
        m[:, r * 512:(r + 1) * 512] = np.where(qq >= kk + r * 128, 0.0, NEGBIG)
    c["causal"] = m.astype(BF)
    pos = np.arange(S, dtype=np.float64)
    j = np.arange(0, HD, 2, dtype=np.float64)
    inv = 1.0 / (10000.0 ** (j / HD))
    ang2 = np.repeat(pos[:, None] * inv[None, :], 2, axis=1)
    cosf = np.cos(ang2).T.astype(np.float32)
    sinf = np.sin(ang2).T.astype(np.float32)
    c["cos2"] = np.vstack([cosf, cosf]).astype(BF)
    c["sin2"] = np.vstack([sinf, sinf]).astype(BF)
    Pm = np.zeros((128, 128), np.float32)
    for jj in range(0, 128, 2):
        Pm[jj, jj + 1] = -1.0
        Pm[jj + 1, jj] = 1.0
    c["perm128"] = Pm.T.copy().astype(BF)
    c["eiota"] = np.tile(np.arange(E, dtype=np.float32), (128, 1)).astype(BF)
    c["eiota16"] = np.tile(np.arange(E, dtype=np.float32), (128, NT)).astype(BF)
    oh = np.zeros((E, E * 128), np.float32)
    for e in range(E):
        oh[e, e * 128:(e + 1) * 128] = 1.0
    c["onehot"] = oh.astype(BF)
    return c


def build(stage="full"):
    p = P()
    nc = p.nc
    nv, ns, nt_, ng = nc.vector, nc.scalar, nc.tensor, nc.gpsimd

    def din(name, shape):
        return nc.dram_tensor(name, list(shape), F32, kind="ExternalInput").ap()

    x_in = din("x", (128, NT * D))
    out = nc.dram_tensor("out", [128, NT * D], F32, kind="ExternalOutput").ap()
    aa = {}
    aa["Wq"] = din("aa_Wq", (128, NC * NC * 128))
    aa["Wk"] = din("aa_Wk", (128, NC * NC * 128))
    aa["Wv"] = din("aa_Wv", (128, NC * D))
    aa["Wo"] = din("aa_Wo", (128, NC * D))
    aa["W1"] = din("aa_W1", (128, 4 * NC * 512))
    aa["W2"] = din("aa_W2", (128, 16 * D))
    for b in ("bq", "bk", "bv", "bo"):
        aa[b] = din("aa_" + b, (1, D))
    aa["b1"] = din("aa_b1", (1, FF))
    aa["b2"] = din("aa_b2", (1, D))
    for v in ("lnq_s", "lnk_s", "lnv_s", "lnm_s", "lnq_b", "lnk_b", "lnv_b", "lnm_b"):
        aa[v] = din("aa_" + v + "_fm", (128, NC))
    ab = {}
    ab["WqT"] = din("ab_WqT", (128, NC * NC * 128))
    ab["WkT"] = din("ab_WkT", (128, NC * NC * 128))
    ab["WvT"] = din("ab_WvT", (128, NC * D))
    ab["WoT"] = din("ab_WoT", (128, NC * D))
    ff = {}
    for g in (0, 1):
        ff[f"r{g}"] = din(f"ff_r{g}", (D, E))
        ff[f"wg{g}"] = din(f"ff_wg{g}", (E, 128, NC * HID))
        ff[f"wu{g}"] = din(f"ff_wu{g}", (E, 128, NC * HID))
        ff[f"wd{g}"] = din(f"ff_wd{g}", (E, 128, 4 * D))

    cnp = _consts_np()
    cdram = {k: nc.inline_tensor(v, name="c_" + k).ap() for k, v in cnp.items()}
    CAP = 256
    wcache = {(g, k): nc.dram_tensor(f"wc{g}{k}", [E, 128, NC * 512], BF16,
                                     kind="Internal").ap()
              for g in (0, 1) for k in range(3)}
    conv_work = []
    for g in (0, 1):
        for k, nm in enumerate(("wg", "wu", "wd")):
            for e0 in range(0, E, 2):
                def cv(g=g, k=k, nm=nm, e0=e0):
                    p.cv.inc(ng.dma_start(wcache[(g, k)][e0:e0 + 2],
                                          ff[f"{nm}{g}"][e0:e0 + 2]), 16)
                conv_work.append(cv)

    def conv_step(n):
        for _ in range(n):
            if conv_work:
                conv_work.pop(0)()
    xs_d = [nc.dram_tensor(f"xs_d{g}", [E * CAP, D], BF16, kind="Internal").ap()
            for g in (0, 1)]
    ys_d = [nc.dram_tensor(f"ys_d{g}", [E * CAP, D], BF16, kind="Internal").ap()
            for g in (0, 1)]

    R = p.sb("R", 0, [128, NT * D], F32)
    STATS = p.sb("STATS", 64 * KB, [128, 96], F32)
    colmask = p.sb("colmask", 64 * KB + 384, [128, 1], F32)
    FMB = p.sb("FMB", 64 * KB + 448, [128, 40], BF16)
    FMV = p.sb("FMV", 64 * KB + 544, [128, 96], F32)
    ident = p.sb("ident", 65 * KB, [128, 128], BF16)
    ones_row = p.sb("ones_row", 65 * KB + 256, [1, 128], BF16)
    ones_row_f32 = p.sb("ones_row_f32", 65 * KB + 512, [1, 128], F32)
    eiota = p.sb("eiota", 64 * KB + 960, [128, E], BF16)
    causal = p.sb("causal", 66 * KB, [128, 4 * 512], BF16)
    onehot = p.sb("onehot", 70 * KB, [E, E * 128], BF16)
    sin2 = p.sb("sin2", 70 * KB, [128, S], BF16)
    perm128 = p.sb("perm128", 74 * KB, [128, 128], BF16)
    cos2 = p.sb("cos2", 78 * KB, [128, S], BF16)
    eiota16 = p.sb("eiota16", 75 * KB, [128, NT * E], BF16)
    ROWS = p.sb("ROWS", 82 * KB, [128, 2048], F32)
    ROWSB = p.sb("ROWSB", 90 * KB, [128, 2048], BF16)
    GP0 = 94 * KB
    A0 = 111 * KB
    A1 = 143 * KB
    A2 = 143 * KB + 33280

    PF = p.stack.enter_context(nc.psum_tensor("PF", [128, 6 * 512], F32))
    PT = p.stack.enter_context(nc.psum_tensor("PT", [128, 2 * 1024], BF16))

    def bank(i):
        return PF[:, i * 512:(i + 1) * 512]

    def bank2(i):
        return PF[:, i * 512:(i + 2) * 512]

    for name, dst in (("ident", ident), ("ones_row", ones_row),
                      ("ones_row_f32", ones_row_f32), ("eiota", eiota),
                      ("eiota16", eiota16),
                      ("colmask", colmask), ("causal", causal)):
        p.d.inc(nc.sync.dma_start(dst[:], cdram[name]), 16)
    p.d.inc(nc.sync.dma_start(R[:], x_in), 16)
    lnfm = {}
    fmoff = 0
    for v in ("lnq_s", "lnk_s", "lnv_s", "lnm_s", "lnq_b", "lnk_b", "lnv_b", "lnm_b"):
        lnfm[v] = FMV[:, fmoff:fmoff + NC]
        p.d.inc(nc.sync.dma_start(lnfm[v], aa[v]), 16)
        fmoff += NC
    ld0 = p.d.n
    A1z = p.sb("A1z", A1, [128, 8192], BF16)
    p.v.inc(nv.memset(A1z[:], 0.0))
    a1z_v = p.v.n
    p.wait(ng, p.v, a1z_v)
    for g in (0, 1):
        xs_r = xs_d[g].rearrange("(a p) w -> p a w", p=128)
        for i in range(8):
            p.xs.inc(ng.dma_start(xs_r[:, i * 8:(i + 1) * 8, :],
                                  A1z[:].rearrange("p (a w) -> p a w", a=8)), 16)
    xs_zero = p.xs.n
    lnfmb = {}
    p.wait(nv, p.d, ld0)
    for i, v in enumerate(("lnq_b", "lnk_b", "lnv_b", "lnm_b")):
        lnfmb[v] = FMB[:, i * NC:(i + 1) * NC]
        p.v.inc(nv.tensor_copy(lnfmb[v], lnfm[v]))
    fmb_v = p.v.n

    def ln_stats(src_tile, sq_trash):
        for t in range(NT):
            src = src_tile(t)
            p.wait(ns, p.s)
            p.s.inc(ns.activation(sq_trash, src, AF.Square,
                                  accum_out=STATS[:, ST_T0 + t:ST_T0 + t + 1]))
            p.v.inc(nv.reduce_sum(STATS[:, ST_MU + t:ST_MU + t + 1], src, axis=AX.X))
        p.wait(nv, p.v)
        p.wait(nv, p.s)
        mu = STATS[:, ST_MU:ST_MU + NT]
        var = STATS[:, ST_T0:ST_T0 + NT]
        tmp = STATS[:, ST_T1:ST_T1 + NT]
        neg = STATS[:, ST_NEG:ST_NEG + NT]
        rstd = STATS[:, ST_RSTD:ST_RSTD + NT]
        p.v.inc(nv.tensor_scalar_mul(mu, mu, 1.0 / D))
        p.wait(nv, p.v)
        p.v.inc(nv.tensor_scalar_mul(neg, mu, -1.0))
        p.v.inc(nv.tensor_mul(tmp, mu, mu))
        p.wait(nv, p.v)
        p.v.inc(nv.tensor_scalar_mul(var, var, 1.0 / D))
        p.wait(nv, p.v)
        p.v.inc(nv.tensor_sub(var, var, tmp))
        p.wait(nv, p.v)
        p.v.inc(nv.tensor_scalar_add(tmp, var, EPS))
        p.wait(ns, p.v)
        p.s.inc(ns.activation(tmp, tmp, AF.Sqrt))
        p.wait(nv, p.s)
        p.v.inc(nv.reciprocal(rstd, tmp))
        p.wait(nv, p.v)

    def tmaj_to_fmaj(src_tile_bf, dstT, tag):
        for t in range(NT):
            half = t % 2
            stg = ROWSB[:, 0:1024] if half == 0 else ROWSB[:, 1024:2048]
            src_tile_bf(t, stg)
            p.wait(nt_, p.v)
            p.wait(nt_, p.s)
            dstp = PT[:, half * 1024:(half + 1) * 1024]
            for c in range(NC):
                p.t.inc(nt_.transpose(dstp[:, c * 128:(c + 1) * 128],
                                      stg[:, c * 128:(c + 1) * 128], ident[:]))
            p.wait(ns, p.t)
            dstv = dstT[:].rearrange("p (c tt q) -> p c tt q", c=NC, tt=NT)[:, :, t, :]
            srcv = dstp.rearrange("p (c q) -> p c q", c=NC)
            p.s.inc(ns.copy(dstv, srcv))

    def load_w_chunks(dst, src_hbm, ncol, ch):
        ch.inc(ng.dma_start(dst[:], src_hbm), 16)

    def bias_row_mms(b_fm_bf, w_sl, ncol, pbank_off, nchunk=NC):
        p.wait(nt_, p.v)
        p.wait(nt_, p.s)
        nblk = (ncol + 511) // 512
        for c in range(nchunk):
            for b in range(nblk):
                w = min(512, ncol - b * 512)
                p.t.inc(nt_.matmul(
                    PF[0:1, pbank_off + b * 512:pbank_off + b * 512 + w],
                    b_fm_bf[:, c:c + 1],
                    w_sl[:, c * ncol + b * 512:c * ncol + b * 512 + w],
                    start=(c == 0), stop=(c == nchunk - 1)))

    def row_to_col(row_bf, col_dst_f32):
        p.wait(nt_, p.s)
        p.t.inc(nt_.matmul(PF[0:128, 5 * 512:5 * 512 + 1], row_bf,
                           ones_row[0:1, 0:1], start=True, stop=True))
        p.wait(ns, p.t)
        p.s.inc(ns.copy(col_dst_f32, PF[0:128, 5 * 512:5 * 512 + 1]))

    xhat2T = p.sb("xhat2T", A0, [128, NC * S], BF16)
    vball = p.sb("vball", A1, [128, NT * H * 65], BF16)
    attT = p.sb("attT", A2, [128, NC * S], BF16)
    vW = p.sb("vW", A2, [128, NC * D], BF16)
    qTr = p.sb("qTr", GP0, [128, S], BF16)
    kTr = p.sb("kTr", GP0 + 4 * KB, [128, S], BF16)
    wq_sl = p.sb("wq_sl", GP0 + 8 * KB, [128, NC * 128], BF16)
    wk_sl = p.sb("wk_sl", GP0 + 10 * KB, [128, NC * 128], BF16)
    pu2 = [p.sb("pu20", GP0 + 12 * KB, [128, 1024], BF16),
           p.sb("pu21", GP0 + 14 * KB, [128, 1024], BF16)]
    rsinv = p.sb("rsinv", GP0 + 10 * KB, [1, 512], F32)
    rsbc = p.sb("rsbc", GP0 + 8 * KB, [128, 512], F32)

    p.wait(ns, p.d, ld0)
    p.wait(nv, p.d, ld0)

    ln_stats(lambda t: R[:, t * D:(t + 1) * D],
             sq_trash=ROWS[:, 0:1024])
    var1 = STATS[:, ST_T0:ST_T0 + NT]
    t1 = STATS[:, ST_T1:ST_T1 + NT]
    rp = STATS[:, ST_RP:ST_RP + NT]
    rstd1 = STATS[:, ST_RSTD:ST_RSTD + NT]
    p.v.inc(nv.tensor_mul(t1, var1, rstd1))
    p.wait(nv, p.v)
    p.v.inc(nv.tensor_mul(t1, t1, rstd1))
    p.wait(nv, p.v)
    p.v.inc(nv.tensor_scalar_add(t1, t1, EPS))
    p.wait(ns, p.v)
    p.s.inc(ns.activation(t1, t1, AF.Sqrt))
    p.wait(nv, p.s)
    p.v.inc(nv.reciprocal(t1, t1))
    p.wait(nv, p.v)
    p.v.inc(nv.tensor_mul(rp, rstd1, t1))
    p.wait(nv, p.v)

    def mk_xhat2(t, stg):
        p.wait(nv, p.t)
        p.v.inc(nv.tensor_scalar(stg, R[:, t * D:(t + 1) * D],
                                 STATS[:, ST_NEG + t:ST_NEG + t + 1],
                                 STATS[:, ST_RP + t:ST_RP + t + 1],
                                 ALU.add, ALU.mult))
    tmaj_to_fmaj(mk_xhat2, xhat2T, "xh2")
    xh2_s = p.s.n

    load_w_chunks(vW, aa["Wv"], D, p.w[0])
    p.wait(nt_, p.w[0])
    p.wait(nt_, p.v, fmb_v)
    bias_row_mms(lnfmb["lnv_b"], vW, D, 0)
    bias_t = p.t.n
    p.wait(nc.sync, p.s)
    p.d.inc(nc.sync.dma_start(ROWS[0:1, 0:D], aa["bv"]), 16)
    p.wait(nv, p.d)
    p.wait(nv, p.t, bias_t)
    p.v.inc(nv.tensor_add(ROWSB[0:1, 0:D], PF[0:1, 0:D], ROWS[0:1, 0:D]))
    p.wait(nt_, p.v)
    for b_ in range(2):
        p.t.inc(nt_.matmul(bank(2 + b_), ones_row[:],
                           ROWSB[0:1, b_ * 512:(b_ + 1) * 512], start=True, stop=True))
    p.wait(ns, p.t)
    p.s.inc(ns.copy(ROWSB[:, 1024:2048], PF[:, 2 * 512:4 * 512]))
    vbias_s = p.s.n
    p.wait(nv, p.t, bias_t)
    for c in range(NC):
        p.v.inc(nv.tensor_scalar_mul(vW[:, c * D:(c + 1) * D],
                                     vW[:, c * D:(c + 1) * D],
                                     lnfm["lnv_s"][:, c:c + 1]))
    vw_scaled_v = p.v.n
    p.wait(nv, p.xs, xs_zero)
    p.v.inc(nv.memset(vball[:], 1.0))
    vb_ones_v = p.v.n
    vb_r = vball[:].rearrange("p (t h c) -> p t h c", t=NT, h=H)
    for t in range(NT):
        pb = 0 if (t % 2 == 0) else 2
        p.wait(nt_, p.s, max(xh2_s, vbias_s))
        p.wait(nt_, p.v, max(vw_scaled_v, vb_ones_v))
        if t >= 2:
            p.wait(nt_, p.v)
        for c in range(NC):
            for b_ in range(2):
                p.t.inc(nt_.matmul(
                    bank(pb + b_),
                    xhat2T[:, c * S + t * 128:c * S + (t + 1) * 128],
                    vW[:, c * D + b_ * 512:c * D + (b_ + 1) * 512],
                    start=(c == 0), stop=(c == NC - 1)))
        p.wait(nv, p.t)
        p.wait(nv, p.s, vbias_s)
        p.v.inc(nv.tensor_add(
            vb_r[:, t, :, 0:64],
            bank2(pb).rearrange("p (h c) -> p h c", h=H),
            ROWSB[:, 1024:2048].rearrange("p (h c) -> p h c", h=H)))
    v_done_v = p.v.n

    p.wait(nc.sync, p.v)
    p.d.inc(nc.sync.dma_start(ROWS[0:1, 0:D], aa["bq"]), 16)
    p.d.inc(nc.sync.dma_start(ROWS[0:1, 1024:1024 + D], aa["bk"]), 16)
    bqk_d = p.d.n

    qb_col = FMV[:, 64:65]
    kb_col = FMV[:, 65:66]
    wsl_t = 0
    for hp in range(NC):
        conv_step(7)
        p.wait(ng, p.v, v_done_v)
        p.wait(ng, p.t, wsl_t)
        p.wait(ng, p.v)
        p.w[1].inc(ng.dma_start(
            wq_sl[:], aa["Wq"][:, hp * 1024:(hp + 1) * 1024]), 16)
        p.w[2].inc(ng.dma_start(
            wk_sl[:], aa["Wk"][:, hp * 1024:(hp + 1) * 1024]), 16)
        p.wait(nt_, p.w[1])
        p.wait(nt_, p.w[2])
        p.wait(nt_, p.v, fmb_v)
        bias_row_mms(lnfmb["lnq_b"], wq_sl, 128, 4 * 512)
        p.wait(nv, p.t)
        p.wait(nv, p.d, bqk_d)
        p.v.inc(nv.tensor_add(ROWSB[0:1, 0:128], PF[0:1, 4 * 512:4 * 512 + 128],
                              ROWS[0:1, hp * 128:(hp + 1) * 128]))
        p.wait(nt_, p.v)
        row_to_col(ROWSB[0:1, 0:128], qb_col)
        bias_row_mms(lnfmb["lnk_b"], wk_sl, 128, 4 * 512)
        p.wait(nv, p.t)
        p.v.inc(nv.tensor_add(ROWSB[0:1, 128:256], PF[0:1, 4 * 512:4 * 512 + 128],
                              ROWS[0:1, 1024 + hp * 128:1024 + (hp + 1) * 128]))
        p.wait(nt_, p.v)
        row_to_col(ROWSB[0:1, 128:256], kb_col)
        rowcol_s = p.s.n
        for c in range(NC):
            p.v.inc(nv.tensor_scalar_mul(wq_sl[:, c * 128:(c + 1) * 128],
                                         wq_sl[:, c * 128:(c + 1) * 128],
                                         lnfm["lnq_s"][:, c:c + 1]))
            p.v.inc(nv.tensor_scalar_mul(wk_sl[:, c * 128:(c + 1) * 128],
                                         wk_sl[:, c * 128:(c + 1) * 128],
                                         lnfm["lnk_s"][:, c:c + 1]))
        wsl_v = p.v.n
        for (wsl, dst, bcol) in ((wq_sl, qTr, qb_col), (wk_sl, kTr, kb_col)):
            p.wait(nt_, p.v, wsl_v)
            p.wait(nt_, p.s)
            for qb in range(4):
                for c in range(NC):
                    p.t.inc(nt_.matmul(
                        bank(qb), wsl[:, c * 128:(c + 1) * 128],
                        xhat2T[:, c * S + qb * 512:c * S + (qb + 1) * 512],
                        start=(c == 0), stop=(c == NC - 1)))
            p.wait(ns, p.t)
            p.wait(ns, p.s, rowcol_s)
            for qb in range(4):
                p.s.inc(ns.activation(dst[:, qb * 512:(qb + 1) * 512], bank(qb),
                                      AF.Identity, bias=bcol, scale=1.0))
        proj_s = p.s.n
        wsl_t = p.t.n

        exp_ms = {0: proj_s, 1: proj_s}
        p.wait(nt_, p.s, proj_s)
        for qb in range(4):
            def av_mm(kt):
                pp_ = kt % 2
                p.wait(nt_, p.s, exp_ms[pp_])
                p.t.inc(nt_.matmul(
                    PF[0:65, 0:512],
                    vball[:, kt * H * 65 + 2 * hp * 65:kt * H * 65 + (2 * hp + 1) * 65],
                    pu2[pp_][:, 0:512], start=(kt == 0), stop=(kt == NT - 1)))
                p.t.inc(nt_.matmul(
                    PF[0:65, 512:1024],
                    vball[:, kt * H * 65 + (2 * hp + 1) * 65:kt * H * 65 + (2 * hp + 2) * 65],
                    pu2[pp_][:, 512:1024], start=(kt == 0), stop=(kt == NT - 1)))
            for kt in range(NT):
                pp_ = kt % 2
                scb = 2 + 2 * pp_
                p.wait(nt_, p.s, exp_ms[pp_])
                p.t.inc(nt_.matmul(
                    bank(scb), kTr[0:64, kt * 128:(kt + 1) * 128],
                    qTr[0:64, qb * 512:(qb + 1) * 512],
                    start=True, stop=True, tile_position=(0, 0)))
                p.t.inc(nt_.matmul(
                    bank(scb + 1), kTr[64:128, kt * 128:(kt + 1) * 128],
                    qTr[64:128, qb * 512:(qb + 1) * 512],
                    start=True, stop=True, tile_position=(64, 0)))
                sc_t = p.t.n
                if kt == NT - 1 and qb == 0:
                    p.wait(nv, p.t, sc_t)
                    p.v.inc(nv.tensor_add(PF[:, scb * 512:scb * 512 + 1],
                                          PF[:, scb * 512:scb * 512 + 1], colmask[:]))
                    p.v.inc(nv.tensor_add(PF[:, (scb + 1) * 512:(scb + 1) * 512 + 1],
                                          PF[:, (scb + 1) * 512:(scb + 1) * 512 + 1],
                                          colmask[:]))
                    p.wait(ns, p.v)
                p.wait(ns, p.t, sc_t)
                exp_ms[pp_] = p.s.inc(
                    ns.activation(pu2[pp_][:], PF[:, scb * 512:(scb + 2) * 512],
                                  AF.Exp, scale=1.0 / 32.0))
                if kt >= 1:
                    av_mm(kt - 1)
            av_mm(NT - 1)
            av_t = p.t.n
            for hh in range(2):
                base = hh * 64
                p.wait(nv, p.t, av_t)
                p.v.inc(nv.reciprocal(rsinv[:], PF[64:65, hh * 512:hh * 512 + 512]))
                p.wait(nt_, p.v)
                p.wait(nt_, p.s, exp_ms[0])
                p.t.inc(nt_.matmul(bank(2), ones_row_f32[:], rsinv[:],
                                   start=True, stop=True))
                p.wait(ns, p.t)
                p.wait(ns, p.v)
                exp_ms[0] = p.s.inc(ns.copy(rsbc[:], bank(2)))
                p.wait(nv, p.s)
                p.v.inc(nv.tensor_mul(
                    attT[base:base + 64, hp * S + qb * 512:hp * S + (qb + 1) * 512],
                    PF[0:64, hh * 512:hh * 512 + 512], rsbc[0:64, :]))
            p.wait(nt_, p.v)

    woW = p.sb("woW", GP0, [128, NC * D], BF16)
    p.wait(ng, p.s)
    p.wait(ng, p.v)
    p.wait(ng, p.t)
    load_w_chunks(woW, aa["Wo"], D, p.w[0])
    p.wait(nc.sync, p.v)
    p.d.inc(nc.sync.dma_start(ROWS[0:1, 0:D], aa["bo"]), 16)
    p.wait(nv, p.d)
    p.v.inc(nv.tensor_copy(ROWSB[0:1, 0:D], ROWS[0:1, 0:D]))
    bo_v = p.v.n
    p.wait(nt_, p.w[0])
    h_bf = p.sb("h_bf", A1, [128, NT * D], BF16)
    for t in range(NT):
        pb = 0 if (t % 2 == 0) else 2
        p.wait(nt_, p.v)
        p.wait(nt_, p.v, bo_v)
        for c in range(NC):
            for b_ in range(2):
                p.t.inc(nt_.matmul(
                    bank(pb + b_),
                    attT[:, c * S + t * 128:c * S + (t + 1) * 128],
                    woW[:, c * D + b_ * 512:c * D + (b_ + 1) * 512],
                    start=(c == 0), stop=False))
        for b_ in range(2):
            p.t.inc(nt_.matmul(bank(pb + b_), ones_row[:],
                               ROWSB[0:1, b_ * 512:(b_ + 1) * 512],
                               start=False, stop=True))
        wo_t = p.t.n
        xa = ROWS[:, 1024:2048][:, 0:D]
        p.wait(nv, p.v)
        p.wait(nv, p.s)
        p.v.inc(nv.tensor_scalar(xa, R[:, t * D:(t + 1) * D],
                                 STATS[:, ST_NEG + t:ST_NEG + t + 1],
                                 STATS[:, ST_RSTD + t:ST_RSTD + t + 1],
                                 ALU.add, ALU.mult))
        p.wait(nv, p.t, wo_t)
        p.wait(nv, p.v)
        p.v.inc(nv.tensor_add(xa, bank2(pb), xa))
        p.wait(nv, p.v)
        p.v.inc(nv.tensor_add(R[:, t * D:(t + 1) * D],
                              R[:, t * D:(t + 1) * D], xa))
        p.wait(ns, p.v)
        p.s.inc(ns.copy(h_bf[:, t * D:(t + 1) * D], xa))
    h_done_s = p.s.n
    h_done_v = p.v.n

    p.wait(nv, p.t)
    p.wait(ns, p.t)
    p.wait(nv, p.s, h_done_s)
    ln_stats(lambda t: h_bf[:, t * D:(t + 1) * D], sq_trash=ROWS[:, 0:1024])
    yT = xhat2T

    def mk_y(t, stg):
        p.wait(nv, p.t)
        p.v.inc(nv.tensor_scalar(stg, h_bf[:, t * D:(t + 1) * D],
                                 STATS[:, ST_NEG + t:ST_NEG + t + 1],
                                 STATS[:, ST_RSTD + t:ST_RSTD + t + 1],
                                 ALU.add, ALU.mult))
    tmaj_to_fmaj(mk_y, yT, "yT")
    yT_s = p.s.n

    xmT = p.sb("xmT", A0, [128, NC * S], BF16)
    MEbuf = p.sb("MEbuf", A0, [E, S], F32)
    gstb = p.sb("gstb", A0, [128, 2048], BF16)
    xmB = p.sb("xmB", A1, [128, NT * D], BF16)
    ws = [p.sb(f"ws{i}", A2 + i * 8 * KB, [128, NC * 512], BF16) for i in range(4)]
    hi_stage = p.sb("hi_stage", GP0, [128, 1024], BF16)
    loT = p.sb("loT", GP0 + 2 * KB, [128, 1024], BF16)
    rankm1 = p.sb("rankm1", GP0 + 4 * KB, [E, S], BF16)
    rhl = p.sb("rhl", GP0 + 4 * KB, [128, 1024], BF16)
    wTg = [p.sb("wTg0", 78 * KB, [E, S], BF16),
           p.sb("wTg1", 70 * KB, [E, S], BF16)]
    stg_e = [p.sb(f"stg_e{i}", GP0 + i * 4 * KB, [128, 2 * D], BF16) for i in (0, 1)]
    xsTe = [p.sb(f"xsTe{i}", GP0 + 8 * KB + i * 4 * KB, [128, NC * CAP], BF16)
            for i in (0, 1)]
    ysst = [ROWSB[:, 0:D], ROWSB[:, D:2 * D]]
    DESTI = p.sb("DESTI", GP0 + 16 * KB, [128, 8], mybir.dt.int32)

    w1q = p.sb("w1q", GP0, [128, NC * 512], BF16)
    w2q = p.sb("w2q", GP0 + 8 * KB, [128, 4 * D], BF16)
    ghT = p.sb("ghT", A2, [128, 4 * S], BF16)
    hpre = p.sb("hpre", A2 + 16 * KB, [128, 2048], F32)
    gel1 = p.sb("gel1", A2 + 24 * KB, [128, 2048], F32)

    rst = {"evac": 0, "rhl_v": 0, "wT_s": 0}

    def rt_prep():
        p.wait(nc.sync, p.s)
        p.wait(nc.sync, p.v)
        for g in (0, 1):
            p.d.inc(nc.sync.dma_start(
                ROWS[:, 256 + g * 256:256 + (g + 1) * 256]
                .rearrange("p (c e) -> p c e", c=NC),
                ff[f"r{g}"].rearrange("(c p) e -> p c e", p=128)), 16)
        rt_d = p.d.n
        p.wait(nv, p.d, rt_d)
        p.wait(nv, p.t)
        p.wait(nv, p.s)
        for g in (0, 1):
            rf = ROWS[:, 256 + g * 256:256 + (g + 1) * 256]
            p.v.inc(nv.tensor_copy(rhl[:, g * 256:(g + 1) * 256], rf))
        p.wait(nv, p.v)
        for g in (0, 1):
            rf = ROWS[:, 256 + g * 256:256 + (g + 1) * 256]
            rhi = rhl[:, g * 256:(g + 1) * 256]
            p.v.inc(nv.tensor_sub(rhl[:, 512 + g * 256:512 + (g + 1) * 256],
                                  rf, rhi))
        rst["rhl_v"] = p.v.n
        rst["evac"] = p.s.n

    def rt_stats(t, trash):
        src = R[:, t * D:(t + 1) * D]
        p.wait(ns, p.s)
        p.wait(ns, p.v)
        p.s.inc(ns.activation(trash, src, AF.Square,
                              accum_out=STATS[:, ST_T0 + t:ST_T0 + t + 1]))
        p.wait(nv, p.v)
        p.v.inc(nv.reduce_sum(STATS[:, ST_MU + t:ST_MU + t + 1], src, axis=AX.X))

    def rt_fin(t):
        mu = STATS[:, ST_MU + t:ST_MU + t + 1]
        var = STATS[:, ST_T0 + t:ST_T0 + t + 1]
        tmp = STATS[:, ST_T1 + t:ST_T1 + t + 1]
        neg = STATS[:, ST_NEG + t:ST_NEG + t + 1]
        rstd = STATS[:, ST_RSTD + t:ST_RSTD + t + 1]
        p.wait(nv, p.v)
        p.wait(nv, p.s)
        p.v.inc(nv.tensor_scalar(neg, mu, -1.0 / D, None, ALU.mult))
        p.v.inc(nv.tensor_scalar(tmp, mu, 1.0 / D, None, ALU.mult))
        p.wait(nv, p.v)
        p.v.inc(nv.tensor_mul(tmp, tmp, tmp))
        p.v.inc(nv.tensor_scalar(var, var, 1.0 / D, EPS, ALU.mult, ALU.add))
        p.wait(nv, p.v)
        p.v.inc(nv.tensor_sub(var, var, tmp))
        p.wait(ns, p.v)
        p.s.inc(ns.activation(var, var, AF.Sqrt))
        p.wait(nv, p.s)
        p.v.inc(nv.reciprocal(rstd, var))

    def rt_fin_all():
        p.wait(nv, p.v)
        p.wait(nv, p.s)
        mu = STATS[:, ST_MU:ST_MU + NT]
        var = STATS[:, ST_T0:ST_T0 + NT]
        tmp = STATS[:, ST_T1:ST_T1 + NT]
        neg = STATS[:, ST_NEG:ST_NEG + NT]
        rstd = STATS[:, ST_RSTD:ST_RSTD + NT]
        p.v.inc(nv.tensor_scalar_mul(mu, mu, 1.0 / D))
        p.wait(nv, p.v)
        p.v.inc(nv.tensor_scalar_mul(neg, mu, -1.0))
        p.v.inc(nv.tensor_mul(tmp, mu, mu))
        p.wait(nv, p.v)
        p.v.inc(nv.tensor_scalar_mul(var, var, 1.0 / D))
        p.wait(nv, p.v)
        p.v.inc(nv.tensor_sub(var, var, tmp))
        p.wait(nv, p.v)
        p.v.inc(nv.tensor_scalar_add(tmp, var, EPS))
        p.wait(ns, p.v)
        p.s.inc(ns.activation(tmp, tmp, AF.Sqrt))
        p.wait(nv, p.s)
        p.v.inc(nv.reciprocal(rstd, tmp))
        p.wait(nv, p.v)

    def rt_chain(t):
        xm = ROWS[:, 1024:2048]
        p.wait(nv, p.v)
        p.wait(nv, p.t)
        p.v.inc(nv.tensor_scalar(xm, R[:, t * D:(t + 1) * D],
                                 STATS[:, ST_NEG + t:ST_NEG + t + 1],
                                 STATS[:, ST_RSTD + t:ST_RSTD + t + 1],
                                 ALU.add, ALU.mult))
        p.wait(nv, p.v)
        hi = xmB[:, t * D:(t + 1) * D]
        p.v.inc(nv.tensor_copy(hi, xm))
        p.wait(nt_, p.v)
        p.wait(nt_, p.s, rst["evac"])
        for c in range(NC):
            p.t.inc(nt_.transpose(PT[:, c * 128:(c + 1) * 128],
                                  hi[:, c * 128:(c + 1) * 128], ident[:]))
        hi_t = p.t.n
        p.wait(ns, p.t, hi_t)
        dstv = xmT[:].rearrange("p (c tt q) -> p c tt q", c=NC, tt=NT)[:, :, t, :]
        p.s.inc(ns.copy(dstv, PT[:, 0:1024].rearrange("p (c q) -> p c q", c=NC)))
        hi_evac = p.s.n
        p.wait(nv, p.t, hi_t)
        p.v.inc(nv.tensor_sub(hi_stage[:], xm, hi))
        p.wait(nt_, p.v)
        for c in range(NC):
            p.t.inc(nt_.transpose(PT[:, 1024 + c * 128:1024 + (c + 1) * 128],
                                  hi_stage[:, c * 128:(c + 1) * 128], ident[:]))
        lo_t = p.t.n
        p.wait(ns, p.t, lo_t)
        p.s.inc(ns.copy(loT[:].rearrange("p (c q) -> p c q", c=NC),
                        PT[:, 1024:2048].rearrange("p (c q) -> p c q", c=NC)))
        rst["evac"] = p.s.n
        p.wait(nt_, p.s, rst["evac"])
        p.wait(nt_, p.v, rst["rhl_v"])
        xtile = xmT[:].rearrange("p (c tt q) -> p c tt q", c=NC, tt=NT)
        for g in (0, 1):
            rhi = rhl[:, g * 256:(g + 1) * 256]
            rlo = rhl[:, 512 + g * 256:512 + (g + 1) * 256]
            first = True
            for c in range(NC):
                lhs = xtile[:, c, t, :]
                p.t.inc(nt_.matmul(PF[:, g * 512:g * 512 + E], lhs,
                                   rhi[:, c * E:(c + 1) * E],
                                   start=first, stop=False))
                first = False
            for c in range(NC):
                lhs = xtile[:, c, t, :]
                p.t.inc(nt_.matmul(PF[:, g * 512:g * 512 + E], lhs,
                                   rlo[:, c * E:(c + 1) * E],
                                   start=False, stop=False))
            for c in range(NC):
                p.t.inc(nt_.matmul(PF[:, g * 512:g * 512 + E],
                                   loT[:, c * 128:(c + 1) * 128],
                                   rhi[:, c * E:(c + 1) * E],
                                   start=False, stop=(c == NC - 1)))
        lg_t = p.t.n
        CB = [0, 200]
        lg_ = [PF[:, 0:E], PF[:, 512:512 + E]]
        ptmp = [ROWS[:, CB[g]:CB[g] + E] for g in (0, 1)]
        msk = [ROWS[:, CB[g] + 32:CB[g] + 64] for g in (0, 1)]
        pm = [ROWS[:, CB[g] + 64:CB[g] + 96] for g in (0, 1)]
        m1n = [ROWS[:, CB[g] + 96:CB[g] + 97] for g in (0, 1)]
        sume = [ROWS[:, CB[g] + 97:CB[g] + 98] for g in (0, 1)]
        rec = [ROWS[:, CB[g] + 98:CB[g] + 99] for g in (0, 1)]
        m2 = [ROWS[:, CB[g] + 99:CB[g] + 100] for g in (0, 1)]
        p.wait(nv, p.t, lg_t)
        p.wait(nv, p.s)
        for g in (0, 1):
            p.v.inc(nv.reduce_max(m1n[g], lg_[g], axis=AX.X, negate=True))
        p.wait(ns, p.v)
        for g in (0, 1):
            p.s.inc(ns.activation(ptmp[g], lg_[g], AF.Exp, bias=m1n[g],
                                  accum_out=sume[g]))
        p.wait(nv, p.s)
        for g in (0, 1):
            p.v.inc(nv.reciprocal(rec[g], sume[g]))
        p.wait(nv, p.v)
        for g in (0, 1):
            p.v.inc(nv.tensor_scalar_mul(ptmp[g], ptmp[g], rec[g]))
        p.wait(nv, p.v)
        for g in (0, 1):
            p.v.inc(nv.tensor_scalar(msk[g], ptmp[g], rec[g], None, ALU.is_ge))
        p.wait(nv, p.v)
        for g in (0, 1):
            p.v.inc(nv.scalar_tensor_tensor(pm[g], msk[g], -3.0, ptmp[g],
                                            ALU.mult, ALU.add))
        p.wait(nv, p.v)
        for g in (0, 1):
            p.v.inc(nv.reduce_max(m2[g], pm[g], axis=AX.X))
        p.wait(nv, p.v)
        for g in (0, 1):
            p.v.inc(nv.tensor_mul(
                ROWSB[:, g * 1024 + t * 32:g * 1024 + (t + 1) * 32],
                msk[g], ptmp[g]))
            p.v.inc(nv.tensor_scalar(pm[g], ptmp[g], m2[g], None, ALU.is_ge))
        p.wait(nv, p.v)
        for g in (0, 1):
            p.v.inc(nv.tensor_sub(pm[g], pm[g], msk[g]))
        p.wait(nv, p.v)
        for g in (0, 1):
            p.v.inc(nv.tensor_mul(
                ROWSB[:, g * 1024 + 512 + t * 32:g * 1024 + 512 + (t + 1) * 32],
                pm[g], ptmp[g]))


    def rt_wT(t):
        for g in (0, 1):
            p.wait(nv, p.t)
            p.wait(nv, p.v)
            wu_st = hi_stage[:, 0:E]
            p.v.inc(nv.tensor_add(
                wu_st,
                ROWSB[:, g * 1024 + t * 32:g * 1024 + (t + 1) * 32],
                ROWSB[:, g * 1024 + 512 + t * 32:g * 1024 + 512 + (t + 1) * 32]))
            p.wait(nt_, p.v)
            p.wait(nt_, p.s)
            p.t.inc(nt_.transpose(
                PT[0:E, 0:128], wu_st, ident[:]))
            p.wait(ns, p.t)
            p.s.inc(ns.copy(wTg[g][:, t * 128:(t + 1) * 128], PT[0:E, 0:128]))
        rst["evac"] = p.s.n
        rst["wT_s"] = p.s.n

    p.wait(nc.sync, p.v)
    p.wait(nc.sync, p.s)
    p.d.inc(nc.sync.dma_start(ROWS[0:1, 0:FF], aa["b1"]), 16)
    p.d.inc(nc.sync.dma_start(ROWS[32:33, 0:D], aa["b2"]), 16)
    b12_d = p.d.n
    b1c = FMV[:, 66:70]
    w2_add_v = 0
    for qh in range(4):
        conv_step(6)
        p.wait(ng, p.t)
        p.w[1].inc(ng.dma_start(
            w1q[:], aa["W1"][:, qh * 4096:(qh + 1) * 4096]), 16)
        p.w[2].inc(ng.dma_start(
            w2q[:], aa["W2"][:, qh * 4096:(qh + 1) * 4096]), 16)
        p.wait(nt_, p.w[1])
        p.wait(nt_, p.w[2])
        p.wait(nt_, p.v, fmb_v)
        p.wait(nt_, p.v)
        bias_row_mms(lnfmb["lnm_b"], w1q, 512, 4 * 512)
        p.wait(nv, p.t)
        p.wait(nv, p.d, b12_d)
        p.v.inc(nv.tensor_add(ROWSB[0:1, 0:512], PF[0:1, 4 * 512:5 * 512],
                              ROWS[0:1, qh * 512:(qh + 1) * 512]))
        p.wait(nt_, p.v)
        for cc in range(4):
            row_to_col(ROWSB[0:1, cc * 128:(cc + 1) * 128], b1c[:, cc:cc + 1])
        b1c_s = p.s.n
        for c in range(NC):
            p.v.inc(nv.tensor_scalar_mul(w1q[:, c * 512:(c + 1) * 512],
                                         w1q[:, c * 512:(c + 1) * 512],
                                         lnfm["lnm_s"][:, c:c + 1]))
        w1s_v = p.v.n
        hpre_ms = 0
        for hc in range(4):
            p.wait(nt_, p.v, w1s_v)
            p.wait(nt_, p.s, max(hpre_ms, b1c_s))
            for tb in range(4):
                for c in range(NC):
                    p.t.inc(nt_.matmul(
                        bank(tb), w1q[:, c * 512 + hc * 128:c * 512 + (hc + 1) * 128],
                        yT[:, c * S + tb * 512:c * S + (tb + 1) * 512],
                        start=(c == 0), stop=(c == NC - 1)))
            mm_t = p.t.n
            pfull = PF[:, 0:2048]
            p.wait(ns, p.t, mm_t)
            p.wait(ns, p.s, b1c_s)
            p.wait(ns, p.v)
            hpre_ms = p.s.inc(
                ns.activation(hpre[:], pfull, AF.Identity, bias=b1c[:, hc:hc + 1]))
            p.wait(nv, p.s)
            p.v.inc(nv.tensor_mul(gel1[:], hpre[:], hpre[:]))
            p.wait(nv, p.v)
            p.v.inc(nv.tensor_scalar(gel1[:], gel1[:], GELU_C, 1.0, ALU.mult, ALU.add))
            p.wait(nv, p.v)
            p.v.inc(nv.tensor_mul(gel1[:], gel1[:], hpre[:]))
            p.wait(ns, p.v)
            p.s.inc(ns.activation(gel1[:], gel1[:], AF.Tanh, scale=GELU_S))
            p.wait(nv, p.s)
            p.v.inc(nv.tensor_scalar(gel1[:], gel1[:], 1.0, 0.5, ALU.add, ALU.mult))
            p.wait(nv, p.v)
            p.v.inc(nv.tensor_mul(ghT[:, hc * S:(hc + 1) * S], gel1[:], hpre[:]))
        gh_v = p.v.n
        p.wait(nt_, p.v, gh_v)
        if qh == 0:
            p.wait(nv, p.d, b12_d)
            p.v.inc(nv.tensor_copy(ROWSB[0:1, 1024:1024 + D], ROWS[32:33, 0:D]))
            p.wait(nt_, p.v)
        rt_on = (qh == 3 and stage != "A")
        if rt_on:
            rt_prep()
        for t in range(NT):
            if w2_add_v:
                p.wait(nt_, p.v, w2_add_v)
            for hc in range(4):
                for b_ in range(2):
                    p.t.inc(nt_.matmul(
                        bank(4 + b_),
                        ghT[:, hc * S + t * 128:hc * S + (t + 1) * 128],
                        w2q[:, hc * D + b_ * 512:hc * D + (b_ + 1) * 512],
                        start=(hc == 0), stop=(hc == 3 and qh != 0)))
            if qh == 0:
                for b_ in range(2):
                    p.t.inc(nt_.matmul(bank(4 + b_), ones_row[:],
                                       ROWSB[0:1, 1024 + b_ * 512:1024 + (b_ + 1) * 512],
                                       start=False, stop=True))
            w2mm_t = p.t.n
            p.wait(nv, p.t, w2mm_t)
            p.v.inc(nv.tensor_add(R[:, t * D:(t + 1) * D],
                                  R[:, t * D:(t + 1) * D], bank2(4)))
            w2_add_v = p.v.n
            if rt_on:
                rt_stats(t, hpre[:, 0:1024])
                if t >= 1:
                    rt_fin(t - 1)
                    rt_chain(t - 1)
                    rt_wT(t - 1)
        if rt_on:
            rt_fin(NT - 1)
            rt_chain(NT - 1)
            rt_wT(NT - 1)

    if stage == "A":
        _emit_out(p, R, out)
        p.stack.close()
        return nc




    def emit_moe(tag, pre=False):
        for eng in (nv, ns, nt_, ng, nc.sync):
            p.wait(eng, p.v)
            p.wait(eng, p.s)
            p.wait(eng, p.t)

        conv_step(len(conv_work))
        if not pre:
            for t in range(NT):
                rt_stats(t, ROWS[:, 0:1024])
            rt_fin_all()
            rt_prep()
            for t in range(NT):
                rt_chain(t)
            for t in range(NT):
                rt_wT(t)
        wT_s = rst["wT_s"]

        for g in (0, 1):
            gb = 100 + g * 96
            p.wait(nv, p.s, wT_s)
            p.wait(nv, p.t)
            me = MEbuf[:]
            p.v.inc(nv.tensor_scalar(me, wTg[g][:], 0.0, None, ALU.is_gt))
            p.wait(nv, p.v)
            p.v.inc(nv.tensor_tensor_scan(me, me, me, 0.0, ALU.add, ALU.bypass))
            p.wait(nv, p.v)
            p.v.inc(nv.tensor_scalar(rankm1[:], me, -1.0, None, ALU.add))
            rk_v = p.v.n
            p.wait(nt_, p.v, rk_v)
            p.wait(nt_, p.s)
            for t in range(NT):
                p.t.inc(nt_.transpose(PT[:, t * E:(t + 1) * E],
                                      rankm1[:, t * 128:(t + 1) * 128],
                                      ident[0:E, 0:E]))
            p.wait(ns, p.t)
            p.wait(ns, p.v)
            rkT_all = hi_stage[:, 0:NT * E]
            p.s.inc(ns.copy(rkT_all, PT[:, 0:NT * E]))
            rkT_s = p.s.n
            p.wait(nv, p.s, rkT_s)
            msk_all = ROWS[:, 1024:1536]
            tmp_all = ROWS[:, 1536:2048]
            for si in range(2):
                wall = ROWSB[:, g * 1024 + si * 512:g * 1024 + (si + 1) * 512]
                d16 = ROWS[:, gb + si * 16:gb + si * 16 + 16]
                r16 = ROWS[:, 64:80]
                w16 = ROWS[:, gb + 32 + si * 16:gb + 48 + si * 16]
                p.wait(nv, p.v)
                p.v.inc(nv.tensor_scalar(msk_all, wall, 0.0, None, ALU.is_gt))
                p.wait(nv, p.v)
                p.v.inc(nv.tensor_mul(tmp_all, msk_all, eiota16[:]))
                p.wait(nv, p.v)
                p.v.inc(nv.reduce_sum(
                    d16, tmp_all.rearrange("p (t e) -> p t e", t=NT), axis=AX.X))
                p.wait(nv, p.v)
                p.v.inc(nv.tensor_mul(tmp_all, msk_all, rkT_all))
                p.wait(nv, p.v)
                p.v.inc(nv.reduce_sum(
                    r16, tmp_all.rearrange("p (t e) -> p t e", t=NT), axis=AX.X))
                p.wait(nv, p.v)
                p.v.inc(nv.tensor_scalar(d16, d16, float(CAP), None, ALU.mult))
                p.wait(nv, p.v)
                p.v.inc(nv.tensor_add(d16, d16, r16))
                p.v.inc(nv.reduce_sum(
                    w16, wall.rearrange("p (t e) -> p t e", t=NT), axis=AX.X))
                p.wait(nv, p.v)
                p.v.inc(nv.tensor_scalar_mul(w16, w16, 0.5))
                p.wait(nv, p.v)

        def scatter_closures(g):
            gb = 100 + g * 96
            work = []
            for t in range(NT):
                for si in range(2):
                    def go(t=t, si=si, g=g, gb=gb):
                        dcol = ROWS[:, gb + si * 16 + t:gb + si * 16 + t + 1]
                        di = DESTI[:, si:si + 1]
                        p.wait(nv, p.ga[0])
                        p.wait(nv, p.ga[1])
                        p.wait(nv, p.xs)
                        p.v.inc(nv.tensor_copy(di, dcol))
                        p.wait(ng, p.v)
                        p.wait(ng, p.xs, xs_zero)
                        p.xs.inc(ng.indirect_dma_start(
                            xs_d[g], bass.IndirectOffsetOnAxis(di, 0),
                            xmB[:, t * D:(t + 1) * D], None), 16)
                    work.append(go)
            return work

        def gather_closures(g, ys_total):
            gb = 100 + g * 96
            work = []
            for t in range(NT):
                for si in range(2):
                    def go(t=t, si=si, g=g, gb=gb, ys_total=ys_total):
                        gpar = (t * 2 + si) % 2
                        gst = gstb[:, gpar * 1024:(gpar + 1) * 1024]
                        dcol = ROWS[:, gb + si * 16 + t:gb + si * 16 + t + 1]
                        wcol = ROWS[:, gb + 32 + si * 16 + t:gb + 33 + si * 16 + t]
                        di = DESTI[:, 4 + si:5 + si]
                        p.wait(nv, p.ga[0])
                        p.wait(nv, p.ga[1])
                        p.v.inc(nv.tensor_copy(di, dcol))
                        p.wait(ng, p.v)
                        p.wait(ng, p.ys, ys_total)
                        p.ga[gpar].inc(ng.indirect_dma_start(
                            gst, None, ys_d[g],
                            bass.IndirectOffsetOnAxis(di, 0)), 16)
                        p.wait(nv, p.ga[gpar])
                        p.v.inc(nv.scalar_tensor_tensor(
                            R[:, t * D:(t + 1) * D], gst, wcol,
                            R[:, t * D:(t + 1) * D], ALU.mult, ALU.add))
                    work.append(go)
            return work

        bg = []
        for go in scatter_closures(0):
            go()
        for g in (0, 1):
            sc_total = p.xs.n
            wgA, wuA, wdA = wcache[(g, 0)], wcache[(g, 1)], wcache[(g, 2)]
            slot_t = [0, 0, 0, 0]

            def load_slot(i, src3, e):
                p.wait(ng, p.t, slot_t[i])
                p.wait(ng, p.cv)
                p.w[i].inc(ng.dma_start(ws[i][:], src3[e]), 16)

            load_slot(0, wgA, 0)
            load_slot(2, wuA, 0)
            stg_free = [0, 0]

            def issue_stream(e):
                par_ = e % 2
                p.wait(nc.sync, p.xs, sc_total)
                p.wait(nc.sync, p.t, stg_free[par_])
                p.si[par_].inc(nc.sync.dma_start(
                    stg_e[par_][:].rearrange("p (a w) -> p a w", a=2),
                    xs_d[g].rearrange("(a p) w -> p a w", p=128)[:, e * 2:e * 2 + 2, :]), 16)

            issue_stream(0)
            if g == 0:
                bg = scatter_closures(1)
            for e in range(E):
                par = e % 2
                stg = stg_e[par]
                xsT = xsTe[par]
                hT = stg[:, 0:1024]
                gsl = ws[par]
                usl = ws[2 + par]
                sich = p.si[par]
                p.wait(nt_, sich)
                p.wait(nt_, p.s)
                for r in range(2):
                    for c in range(NC):
                        p.t.inc(nt_.transpose(
                            PT[:, r * 1024 + c * 128:r * 1024 + (c + 1) * 128],
                            stg[:, r * D + c * 128:r * D + (c + 1) * 128], ident[:]))
                tr_t = p.t.n
                p.wait(ns, p.t, tr_t)
                dstv = xsT[:].rearrange("p (c r q) -> p c r q", c=NC, r=2)
                p.s.inc(ns.copy(dstv,
                                PT[:].rearrange("p (r c q) -> p c r q", r=2, c=NC)))
                xsT_s = p.s.n
                if e + 1 < E:
                    issue_stream(e + 1)
                p.wait(nt_, p.w[par])
                p.wait(nt_, p.s, xsT_s)
                p.wait(nt_, p.v)
                for ht in range(4):
                    bk = PF[:, ht * 256:(ht + 1) * 256]
                    for c in range(NC):
                        p.t.inc(nt_.matmul(
                            bk, gsl[:, c * 512 + ht * 128:c * 512 + (ht + 1) * 128],
                            xsT[:, c * CAP:(c + 1) * CAP],
                            start=(c == 0), stop=(c == NC - 1)))
                gate_t = p.t.n
                slot_t[par] = gate_t
                load_slot(par, wdA, e)
                p.wait(nt_, p.w[2 + par])
                for ht in range(4):
                    bk = PF[:, 1024 + ht * 256:1024 + (ht + 1) * 256]
                    for c in range(NC):
                        p.t.inc(nt_.matmul(
                            bk, usl[:, c * 512 + ht * 128:c * 512 + (ht + 1) * 128],
                            xsT[:, c * CAP:(c + 1) * CAP],
                            start=(c == 0), stop=(c == NC - 1)))
                up_t = p.t.n
                slot_t[2 + par] = up_t
                if e + 1 < E:
                    load_slot(1 - par, wgA, e + 1)
                    load_slot(3 - par, wuA, e + 1)
                for ht in range(4):
                    p.wait(ns, p.t, gate_t)
                    sg = ROWS[:, 1024:1280]
                    p.wait(ns, p.v)
                    p.s.inc(ns.activation(sg, PF[:, ht * 256:(ht + 1) * 256], AF.Silu))
                    p.wait(nv, p.s)
                    p.wait(nv, p.t, up_t)
                    p.v.inc(nv.tensor_mul(hT[:, ht * 256:(ht + 1) * 256],
                                          PF[:, 1024 + ht * 256:1024 + (ht + 1) * 256],
                                          sg))
                hT_v = p.v.n
                p.wait(nt_, p.v, hT_v)
                p.wait(nt_, p.w[par])
                for r in range(2):
                    p.wait(nt_, p.s)
                    for hc in range(4):
                        for b_ in range(2):
                            p.t.inc(nt_.matmul(
                                bank(4 + b_),
                                hT[:, hc * 256 + r * 128:hc * 256 + (r + 1) * 128],
                                gsl[:, hc * D + b_ * 512:hc * D + (b_ + 1) * 512],
                                start=(hc == 0), stop=(hc == 3)))
                    dn_t = p.t.n
                    p.wait(ns, p.t, dn_t)
                    ysb = ysst[(e * 2 + r) % 2]
                    p.wait(ns, p.ys)
                    p.s.inc(ns.copy(ysb, bank2(4)))
                    p.wait(nc.sync, p.s)
                    p.ys.inc(nc.sync.dma_start(
                        ys_d[g].rearrange("(a p) w -> p a w", p=128)[:, e * 2 + r:e * 2 + r + 1, :],
                        ysb.rearrange("p (a w) -> p a w", a=1)), 16)
                slot_t[par] = p.t.n
                stg_free[par] = p.t.n
                for _ in range(2):
                    if bg:
                        bg.pop(0)()
            ys_total = p.ys.n
            if g == 0:
                for go in bg:
                    go()
                bg = gather_closures(0, ys_total)
            else:
                for go in bg:
                    go()
                for go in gather_closures(1, ys_total):
                    go()

    xbT = p.sb("xbT", A0, [128, NC * S], BF16)

    def emit_attnb():
        for eng in (nv, ns, nt_, ng, nc.sync):
            p.wait(eng, p.v)
            p.wait(eng, p.s)
            p.wait(eng, p.t)
        p.d.inc(nc.sync.dma_start(cos2[:], cdram["cos2"]), 16)
        p.d.inc(nc.sync.dma_start(sin2[:], cdram["sin2"]), 16)
        p.d.inc(nc.sync.dma_start(perm128[:], cdram["perm128"]), 16)
        pc_d = p.d.n
        ones_col = FMB[:, 39:40]
        p.v.inc(nv.memset(ones_col, 1.0))
        oc_v = p.v.n

        ln_stats(lambda t: R[:, t * D:(t + 1) * D], sq_trash=ROWS[:, 0:1024])

        def mk_xb(t, stg):
            p.wait(nv, p.t)
            p.v.inc(nv.tensor_scalar(stg, R[:, t * D:(t + 1) * D],
                                     STATS[:, ST_NEG + t:ST_NEG + t + 1],
                                     STATS[:, ST_RSTD + t:ST_RSTD + t + 1],
                                     ALU.add, ALU.mult))
        tmaj_to_fmaj(mk_xb, xbT, "xb")
        xb_s = p.s.n

        load_w_chunks(vW, ab["WvT"], D, p.w[0])
        p.v.inc(nv.memset(vball[:], 1.0))
        vb_v = p.v.n
        p.wait(nt_, p.w[0])
        vb_r = vball[:].rearrange("p (t h c) -> p t h c", t=NT, h=H)
        for t in range(NT):
            pb = 0 if (t % 2 == 0) else 2
            p.wait(nt_, p.s, xb_s)
            p.wait(nt_, p.v, vb_v)
            p.wait(nt_, p.s)
            for c in range(NC):
                for b_ in range(2):
                    p.t.inc(nt_.matmul(
                        bank(pb + b_),
                        xbT[:, c * S + t * 128:c * S + (t + 1) * 128],
                        vW[:, c * D + b_ * 512:c * D + (b_ + 1) * 512],
                        start=(c == 0), stop=(c == NC - 1)))
            p.wait(ns, p.t)
            p.s.inc(ns.copy(vb_r[:, t, :, 0:64],
                            bank2(pb).rearrange("p (h c) -> p h c", h=H)))
        v_done_s = p.s.n

        def rope_apply(buf, evac_s):
            p.wait(nt_, p.s, evac_s)
            p.wait(nt_, p.v)
            p.wait(nt_, p.d, pc_d)
            for nb in range(4):
                p.t.inc(nt_.matmul(bank(nb), perm128[:],
                                   buf[:, nb * 512:(nb + 1) * 512],
                                   start=True, stop=True))
            pm_t = p.t.n
            p.wait(nv, p.t, pm_t)
            p.v.inc(nv.tensor_mul(ROWSB[:, 0:2048], PF[:, 0:2048], sin2[:]))
            p.wait(nv, p.v)
            p.v.inc(nv.tensor_mul(buf[:], buf[:], cos2[:]))
            p.wait(nv, p.v)
            p.v.inc(nv.tensor_add(buf[:], buf[:], ROWSB[:, 0:2048]))
            p.wait(nv, p.v)

        wsl_tc = 0
        for hp in range(NC):
            conv_step(4)
            p.wait(ng, p.t, wsl_tc)
            p.wait(ng, p.v)
            p.w[1].inc(ng.dma_start(
                wq_sl[:], ab["WqT"][:, hp * 1024:(hp + 1) * 1024]), 16)
            p.w[2].inc(ng.dma_start(
                wk_sl[:], ab["WkT"][:, hp * 1024:(hp + 1) * 1024]), 16)
            p.wait(nt_, p.w[1])
            p.wait(nt_, p.w[2])
            for (wsl, dst) in ((wq_sl, qTr), (wk_sl, kTr)):
                p.wait(nt_, p.s)
                p.wait(nt_, p.v)
                for qb in range(4):
                    for c in range(NC):
                        p.t.inc(nt_.matmul(
                            bank(qb), wsl[:, c * 128:(c + 1) * 128],
                            xbT[:, c * S + qb * 512:c * S + (qb + 1) * 512],
                            start=(c == 0), stop=(c == NC - 1)))
                p.wait(ns, p.t)
                ev = 0
                for qb in range(4):
                    ev = p.s.inc(ns.copy(dst[:, qb * 512:(qb + 1) * 512], bank(qb)))
                rope_apply(dst, ev)
            wsl_tc = p.t.n
            for (buf, col0) in ((qTr, 0), (kTr, 2)):
                p.wait(ns, p.v)
                p.s.inc(ns.activation(ROWSB[:, 0:2048], buf[:], AF.Square))
                p.wait(nt_, p.s)
                p.wait(nt_, p.v, oc_v)
                for hh in range(2):
                    p.wait(nt_, p.v)
                    for nb in range(4):
                        p.t.inc(nt_.matmul(
                            PF[0:1, nb * 512:nb * 512 + 512],
                            ones_col[hh * 64:(hh + 1) * 64],
                            ROWSB[hh * 64:(hh + 1) * 64, 0:2048][:, nb * 512:(nb + 1) * 512],
                            start=True, stop=True))
                    p.wait(nv, p.t)
                    p.v.inc(nv.reduce_max(ROWS[0:1, col0 + hh:col0 + hh + 1],
                                          PF[0:1, 0:2048], axis=AX.X))
            p.wait(nv, p.v)
            p.v.inc(nv.reduce_max(ROWS[0:1, 4:5],
                                  ROWS[0:1, 0:2].rearrange("p (a b) -> p a b", a=1),
                                  axis=AX.X))
            p.wait(nv, p.v)
            p.v.inc(nv.reduce_max(ROWS[0:1, 5:6],
                                  ROWS[0:1, 2:4].rearrange("p (a b) -> p a b", a=1),
                                  axis=AX.X))
            p.wait(ns, p.v)
            p.s.inc(ns.activation(ROWS[0:1, 6:8], ROWS[0:1, 4:6], AF.Sqrt))
            p.wait(nv, p.s)
            p.v.inc(nv.tensor_mul(ROWS[0:1, 8:9], ROWS[0:1, 6:7], ROWS[0:1, 7:8]))
            p.wait(nv, p.v)
            p.v.inc(nv.tensor_scalar_mul(ROWS[0:1, 8:9], ROWS[0:1, 8:9], -0.125))
            p.wait(nt_, p.v)
            p.wait(nt_, p.s)
            p.t.inc(nt_.matmul(PF[0:128, 5 * 512:5 * 512 + 1], ones_row_f32[:],
                               ROWS[0:1, 8:9], start=True, stop=True))
            p.wait(ns, p.t)
            p.s.inc(ns.copy(FMV[:, 70:71], PF[0:128, 5 * 512:5 * 512 + 1]))
            cb_s = p.s.n
            cbcol = FMV[:, 70:71]

            exp_ms = {0: cb_s, 1: cb_s}
            p.wait(nt_, p.s, cb_s)
            p.wait(nt_, p.v)
            for qb in range(4):
                nk = qb * 4 + 4
                def av_mm(kt):
                    pp_ = kt % 2
                    p.wait(nt_, p.s, exp_ms[pp_])
                    p.t.inc(nt_.matmul(
                        PF[0:65, 0:512],
                        vball[:, kt * H * 65 + 2 * hp * 65:kt * H * 65 + (2 * hp + 1) * 65],
                        pu2[pp_][:, 0:512], start=(kt == 0), stop=(kt == nk - 1)))
                    p.t.inc(nt_.matmul(
                        PF[0:65, 512:1024],
                        vball[:, kt * H * 65 + (2 * hp + 1) * 65:kt * H * 65 + (2 * hp + 2) * 65],
                        pu2[pp_][:, 512:1024], start=(kt == 0), stop=(kt == nk - 1)))
                for kt in range(nk):
                    pp_ = kt % 2
                    scb = 2 + 2 * pp_
                    p.wait(nt_, p.s, exp_ms[pp_])
                    p.t.inc(nt_.matmul(
                        bank(scb), kTr[0:64, kt * 128:(kt + 1) * 128],
                        qTr[0:64, qb * 512:(qb + 1) * 512],
                        start=True, stop=True, tile_position=(0, 0)))
                    p.t.inc(nt_.matmul(
                        bank(scb + 1), kTr[64:128, kt * 128:(kt + 1) * 128],
                        qTr[64:128, qb * 512:(qb + 1) * 512],
                        start=True, stop=True, tile_position=(64, 0)))
                    sc_t = p.t.n
                    if kt >= qb * 4:
                        r = kt - qb * 4
                        p.wait(nv, p.t, sc_t)
                        p.v.inc(nv.tensor_add(bank(scb), bank(scb),
                                              causal[:, r * 512:(r + 1) * 512]))
                        p.v.inc(nv.tensor_add(bank(scb + 1), bank(scb + 1),
                                              causal[:, r * 512:(r + 1) * 512]))
                        p.wait(ns, p.v)
                    p.wait(ns, p.t, sc_t)
                    exp_ms[pp_] = p.s.inc(ns.activation(
                        pu2[pp_][:], PF[:, scb * 512:(scb + 2) * 512],
                        AF.Exp, scale=0.125, bias=cbcol))
                    if kt >= 1:
                        av_mm(kt - 1)
                av_mm(nk - 1)
                av_t = p.t.n
                for hh in range(2):
                    base = hh * 64
                    p.wait(nv, p.t, av_t)
                    p.v.inc(nv.reciprocal(rsinv[:], PF[64:65, hh * 512:hh * 512 + 512]))
                    p.wait(nt_, p.v)
                    p.wait(nt_, p.s, exp_ms[0])
                    p.t.inc(nt_.matmul(bank(2), ones_row_f32[:], rsinv[:],
                                       start=True, stop=True))
                    p.wait(ns, p.t)
                    p.wait(ns, p.v)
                    exp_ms[0] = p.s.inc(ns.copy(rsbc[:], bank(2)))
                    p.wait(nv, p.s)
                    p.v.inc(nv.tensor_mul(
                        attT[base:base + 64, hp * S + qb * 512:hp * S + (qb + 1) * 512],
                        PF[0:64, hh * 512:hh * 512 + 512], rsbc[0:64, :]))
                p.wait(nt_, p.v)

        p.wait(ng, p.s)
        p.wait(ng, p.v)
        p.wait(ng, p.t)
        load_w_chunks(woW, ab["WoT"], D, p.w[0])
        p.wait(nt_, p.w[0])
        for t in range(NT):
            pb = 0 if (t % 2 == 0) else 2
            p.wait(nt_, p.v)
            for c in range(NC):
                for b_ in range(2):
                    p.t.inc(nt_.matmul(
                        bank(pb + b_),
                        attT[:, c * S + t * 128:c * S + (t + 1) * 128],
                        woW[:, c * D + b_ * 512:c * D + (b_ + 1) * 512],
                        start=(c == 0), stop=(c == NC - 1)))
            p.wait(nv, p.t)
            p.v.inc(nv.tensor_add(R[:, t * D:(t + 1) * D],
                                  R[:, t * D:(t + 1) * D], bank2(pb)))

    emit_moe("moe1", pre=True)
    if stage == "B":
        _emit_out(p, R, out)
        p.stack.close()
        return nc
    emit_attnb()
    if stage == "C":
        _emit_out(p, R, out)
        p.stack.close()
        return nc
    emit_moe("moe2")
    _emit_out(p, R, out)
    p.stack.close()
    return nc


def _emit_out(p, R, out):
    nc = p.nc
    p.wait(nc.sync, p.v)
    p.wait(nc.sync, p.s)
    p.d.inc(nc.sync.dma_start(out, R[:]), 16)
    p.wait(nc.sync, p.d)


def _chunk(w):
    w = np.asarray(w)
    cc = w.shape[0] // 128
    return np.ascontiguousarray(
        w.reshape(cc, 128, w.shape[1]).transpose(1, 0, 2).reshape(128, -1))


def _pairmaj(w):
    w = np.asarray(w).reshape(8, 128, 8, 128)
    return np.ascontiguousarray(
        w.transpose(1, 2, 0, 3).reshape(128, -1))


def _qmaj(w):
    w = np.asarray(w).reshape(8, 128, 4, 512)
    return np.ascontiguousarray(
        w.transpose(1, 2, 0, 3).reshape(128, -1))


def _core_inputs(inputs, c):
    i = inputs
    fm = lambda v: np.ascontiguousarray(np.asarray(v).reshape(NC, 128).T)
    row = lambda v: np.ascontiguousarray(np.asarray(v).reshape(1, -1))
    xt = np.asarray(i["L0_attn_a_x"])[c].reshape(NT, 128, D)
    m = {
        "x": np.ascontiguousarray(xt.transpose(1, 0, 2).reshape(128, -1)),
        "aa_Wq": _pairmaj(i["L0_attn_a_Wq"]), "aa_Wk": _pairmaj(i["L0_attn_a_Wk"]),
        "aa_Wv": _chunk(i["L0_attn_a_Wv"]), "aa_Wo": _chunk(i["L0_attn_a_Wo"]),
        "aa_W1": _qmaj(i["L0_attn_a_W1"]), "aa_W2": _chunk(i["L0_attn_a_W2"]),
        "aa_bq": row(i["L0_attn_a_bq"]), "aa_bk": row(i["L0_attn_a_bk"]),
        "aa_bv": row(i["L0_attn_a_bv"]), "aa_bo": row(i["L0_attn_a_bo"]),
        "aa_b1": row(i["L0_attn_a_b1"]), "aa_b2": row(i["L0_attn_a_b2"]),
        "aa_lnq_s_fm": fm(i["L0_attn_a_lnq_s"]), "aa_lnq_b_fm": fm(i["L0_attn_a_lnq_b"]),
        "aa_lnk_s_fm": fm(i["L0_attn_a_lnk_s"]), "aa_lnk_b_fm": fm(i["L0_attn_a_lnk_b"]),
        "aa_lnv_s_fm": fm(i["L0_attn_a_lnv_s"]), "aa_lnv_b_fm": fm(i["L0_attn_a_lnv_b"]),
        "aa_lnm_s_fm": fm(i["L0_attn_a_lnm_s"]), "aa_lnm_b_fm": fm(i["L0_attn_a_lnm_b"]),
        "ab_WqT": _pairmaj(np.asarray(i["L0_attn_b_Wq"]).T),
        "ab_WkT": _pairmaj(np.asarray(i["L0_attn_b_Wk"]).T),
        "ab_WvT": _chunk(np.asarray(i["L0_attn_b_Wv"]).T),
        "ab_WoT": _chunk(np.asarray(i["L0_attn_b_Wo"]).T),
    }
    for g in (0, 1):
        m[f"ff_r{g}"] = inputs[f"L0_ffn_router_w{g}"]
        for nm in ("wg", "wu", "wd"):
            w3 = np.asarray(inputs[f"L0_ffn_{nm}{g}"])
            ee, kk, nn2 = w3.shape
            w3 = w3.reshape(ee, kk // 128, 128, nn2).transpose(0, 2, 1, 3)
            m[f"ff_{nm}{g}"] = np.ascontiguousarray(w3.reshape(ee, 128, -1))
    return {k: np.ascontiguousarray(np.asarray(v, dtype=np.float32)) for k, v in m.items()}


def kernel(**inputs):
    nc = build("full")
    in_maps = [_core_inputs(inputs, c) for c in range(8)]
    res = run_bass_kernel_spmd(nc, in_maps, core_ids=list(range(8)))
    outs = [r["out"].reshape(128, NT, D).transpose(1, 0, 2).reshape(S, D)
            for r in res.results]
    return np.stack(outs).astype(np.float32)


def kernel_stage(stage, inputs):
    nc = build(stage)
    in_maps = [_core_inputs(inputs, c) for c in range(8)]
    res = run_bass_kernel_spmd(nc, in_maps, core_ids=list(range(8)))
    outs = [r["out"].reshape(128, NT, D).transpose(1, 0, 2).reshape(S, D)
            for r in res.results]
    return np.stack(outs)


# version 19 (final)
# speedup vs baseline: 1.2147x; 1.2147x over previous
from contextlib import ExitStack

import numpy as np
import ml_dtypes

import concourse.bass as bass
import concourse.mybir as mybir
from concourse.bass_utils import run_bass_kernel_spmd

F32 = mybir.dt.float32
BF16 = mybir.dt.bfloat16
AF = mybir.ActivationFunctionType
ALU = mybir.AluOpType
AX = mybir.AxisListType
BF = ml_dtypes.bfloat16

S, D, H, HD = 2048, 1024, 16, 64
NT, NC = S // 128, D // 128
FF, E, HID = 2048, 32, 512
EPS = 1e-5
GELU_C, GELU_S = 0.044715, 0.7978845608028654
NEGBIG = -2.0e32
KB = 1024

ST_MU, ST_RSTD, ST_NEG, ST_RP, ST_T0, ST_T1 = 0, 16, 32, 48, 64, 80


class Ch:
    def __init__(self, nc, stack, name):
        self.sem = stack.enter_context(nc.semaphore(name))
        self.n = 0

    def inc(self, instr, k=1):
        instr.then_inc(self.sem, k)
        self.n += k
        return self.n


class P:
    def __init__(self):
        nc = bass.Bass("TRN2", target_bir_lowering=False, debug=False)
        self.nc = nc
        self.stack = ExitStack()
        self.v = Ch(nc, self.stack, "chv")
        self.s = Ch(nc, self.stack, "chs")
        self.t = Ch(nc, self.stack, "cht")
        self.g = Ch(nc, self.stack, "chg")
        self.w = [Ch(nc, self.stack, f"chw{i}") for i in range(4)]
        self.xs = Ch(nc, self.stack, "chxs")
        self.ga = [Ch(nc, self.stack, f"chga{i}") for i in range(2)]
        self.si = [Ch(nc, self.stack, f"chsi{i}") for i in range(2)]
        self.ys = Ch(nc, self.stack, "chys")
        self.cv = Ch(nc, self.stack, "chcv")
        self.d = Ch(nc, self.stack, "chd")
        self.gp = Ch(nc, self.stack, "chgp")
        self._lw = {}

    def wait(self, eng, ch, val=None):
        if val is None:
            val = ch.n
        if val <= 0:
            return
        key = (id(eng), id(ch))
        if self._lw.get(key, -1) >= val:
            return
        eng.wait_ge(ch.sem, val)
        self._lw[key] = val

    SB_BASE = 16544

    def sb(self, name, off, shape, dtype):
        return self.nc.alloc_sbuf_tensor_at(name, shape, dtype,
                                            offset=self.SB_BASE + off)


def _consts_np():
    c = {}
    c["ident"] = np.eye(128, dtype=BF)
    c["ones_row"] = np.ones((1, 128), dtype=BF)
    c["ones_row_f32"] = np.ones((1, 128), dtype=np.float32)
    cm = np.zeros((128, 1), np.float32)
    cm[127, 0] = -999.0 * 32.0
    c["colmask"] = cm
    m = np.zeros((128, 4 * 512), np.float32)
    for r in range(4):
        kk = np.arange(128)[:, None]
        qq = np.arange(512)[None, :]
        m[:, r * 512:(r + 1) * 512] = np.where(qq >= kk + r * 128, 0.0, NEGBIG)
    c["causal"] = m.astype(BF)
    pos = np.arange(S, dtype=np.float64)
    j = np.arange(0, HD, 2, dtype=np.float64)
    inv = 1.0 / (10000.0 ** (j / HD))
    ang2 = np.repeat(pos[:, None] * inv[None, :], 2, axis=1)
    cosf = np.cos(ang2).T.astype(np.float32)
    sinf = np.sin(ang2).T.astype(np.float32)
    c["cos2"] = np.vstack([cosf, cosf]).astype(BF)
    c["sin2"] = np.vstack([sinf, sinf]).astype(BF)
    Pm = np.zeros((128, 128), np.float32)
    for jj in range(0, 128, 2):
        Pm[jj, jj + 1] = -1.0
        Pm[jj + 1, jj] = 1.0
    c["perm128"] = Pm.T.copy().astype(BF)
    c["eiota"] = np.tile(np.arange(E, dtype=np.float32), (128, 1)).astype(BF)
    c["eiota16"] = np.tile(np.arange(E, dtype=np.float32), (128, NT)).astype(BF)
    oh = np.zeros((E, E * 128), np.float32)
    for e in range(E):
        oh[e, e * 128:(e + 1) * 128] = 1.0
    c["onehot"] = oh.astype(BF)
    return c


def build(stage="full"):
    p = P()
    nc = p.nc
    nv, ns, nt_, ng = nc.vector, nc.scalar, nc.tensor, nc.gpsimd

    def din(name, shape):
        return nc.dram_tensor(name, list(shape), F32, kind="ExternalInput").ap()

    x_in = din("x", (128, NT * D))
    out = nc.dram_tensor("out", [128, NT * D], F32, kind="ExternalOutput").ap()
    aa = {}
    aa["Wq"] = din("aa_Wq", (128, NC * NC * 128))
    aa["Wk"] = din("aa_Wk", (128, NC * NC * 128))
    aa["Wv"] = din("aa_Wv", (128, NC * D))
    aa["Wo"] = din("aa_Wo", (128, NC * D))
    aa["W1"] = din("aa_W1", (128, 4 * NC * 512))
    aa["W2"] = din("aa_W2", (128, 16 * D))
    for b in ("bq", "bk", "bv", "bo"):
        aa[b] = din("aa_" + b, (1, D))
    aa["b1"] = din("aa_b1", (1, FF))
    aa["b2"] = din("aa_b2", (1, D))
    for v in ("lnq_s", "lnk_s", "lnv_s", "lnm_s", "lnq_b", "lnk_b", "lnv_b", "lnm_b"):
        aa[v] = din("aa_" + v + "_fm", (128, NC))
    ab = {}
    ab["WqT"] = din("ab_WqT", (128, NC * NC * 128))
    ab["WkT"] = din("ab_WkT", (128, NC * NC * 128))
    ab["WvT"] = din("ab_WvT", (128, NC * D))
    ab["WoT"] = din("ab_WoT", (128, NC * D))
    ff = {}
    for g in (0, 1):
        ff[f"r{g}"] = din(f"ff_r{g}", (D, E))
        ff[f"wg{g}"] = din(f"ff_wg{g}", (E, 128, NC * HID))
        ff[f"wu{g}"] = din(f"ff_wu{g}", (E, 128, NC * HID))
        ff[f"wd{g}"] = din(f"ff_wd{g}", (E, 128, 4 * D))

    cnp = _consts_np()
    cdram = {k: nc.inline_tensor(v, name="c_" + k).ap() for k, v in cnp.items()}
    CAP = 256
    wcache = {(g, k): nc.dram_tensor(f"wc{g}{k}", [E, 128, NC * 512], BF16,
                                     kind="Internal").ap()
              for g in (0, 1) for k in range(3)}
    conv_work = []
    for g in (0, 1):
        for k, nm in enumerate(("wg", "wu", "wd")):
            for e0 in range(0, E, 2):
                def cv(g=g, k=k, nm=nm, e0=e0):
                    p.cv.inc(ng.dma_start(wcache[(g, k)][e0:e0 + 2],
                                          ff[f"{nm}{g}"][e0:e0 + 2]), 16)
                conv_work.append(cv)

    def conv_step(n):
        for _ in range(n):
            if conv_work:
                conv_work.pop(0)()
    xs_d = [nc.dram_tensor(f"xs_d{g}", [E * CAP, D], BF16, kind="Internal").ap()
            for g in (0, 1)]
    ys_d = [nc.dram_tensor(f"ys_d{g}", [E * CAP, D], BF16, kind="Internal").ap()
            for g in (0, 1)]

    R = p.sb("R", 0, [128, NT * D], F32)
    STATS = p.sb("STATS", 64 * KB, [128, 96], F32)
    colmask = p.sb("colmask", 64 * KB + 384, [128, 1], F32)
    FMB = p.sb("FMB", 64 * KB + 448, [128, 40], BF16)
    FMV = p.sb("FMV", 64 * KB + 544, [128, 96], F32)
    ident = p.sb("ident", 65 * KB, [128, 128], BF16)
    ones_row = p.sb("ones_row", 65 * KB + 256, [1, 128], BF16)
    ones_row_f32 = p.sb("ones_row_f32", 65 * KB + 512, [1, 128], F32)
    eiota = p.sb("eiota", 64 * KB + 960, [128, E], BF16)
    causal = p.sb("causal", 66 * KB, [128, 4 * 512], BF16)
    onehot = p.sb("onehot", 70 * KB, [E, E * 128], BF16)
    sin2 = p.sb("sin2", 70 * KB, [128, S], BF16)
    perm128 = p.sb("perm128", 74 * KB, [128, 128], BF16)
    cos2 = p.sb("cos2", 78 * KB, [128, S], BF16)
    eiota16 = p.sb("eiota16", 75 * KB, [128, NT * E], BF16)
    ROWS = p.sb("ROWS", 82 * KB, [128, 2048], F32)
    ROWSB = p.sb("ROWSB", 90 * KB, [128, 2048], BF16)
    GP0 = 94 * KB
    A0 = 111 * KB
    A1 = 143 * KB
    A2 = 143 * KB + 33280

    PF = p.stack.enter_context(nc.psum_tensor("PF", [128, 6 * 512], F32))
    PT = p.stack.enter_context(nc.psum_tensor("PT", [128, 2 * 1024], BF16))

    def bank(i):
        return PF[:, i * 512:(i + 1) * 512]

    def bank2(i):
        return PF[:, i * 512:(i + 2) * 512]

    for name, dst in (("ident", ident), ("ones_row", ones_row),
                      ("ones_row_f32", ones_row_f32), ("eiota", eiota),
                      ("eiota16", eiota16),
                      ("colmask", colmask), ("causal", causal)):
        p.d.inc(nc.sync.dma_start(dst[:], cdram[name]), 16)
    p.d.inc(nc.sync.dma_start(R[:], x_in), 16)
    lnfm = {}
    fmoff = 0
    for v in ("lnq_s", "lnk_s", "lnv_s", "lnm_s", "lnq_b", "lnk_b", "lnv_b", "lnm_b"):
        lnfm[v] = FMV[:, fmoff:fmoff + NC]
        p.d.inc(nc.sync.dma_start(lnfm[v], aa[v]), 16)
        fmoff += NC
    ld0 = p.d.n
    A1z = p.sb("A1z", A1, [128, 8192], BF16)
    p.v.inc(nv.memset(A1z[:], 0.0))
    a1z_v = p.v.n
    p.wait(ng, p.v, a1z_v)
    for g in (0, 1):
        xs_r = xs_d[g].rearrange("(a p) w -> p a w", p=128)
        for i in range(8):
            p.xs.inc(ng.dma_start(xs_r[:, i * 8:(i + 1) * 8, :],
                                  A1z[:].rearrange("p (a w) -> p a w", a=8)), 16)
    xs_zero = p.xs.n
    lnfmb = {}
    p.wait(nv, p.d, ld0)
    for i, v in enumerate(("lnq_b", "lnk_b", "lnv_b", "lnm_b")):
        lnfmb[v] = FMB[:, i * NC:(i + 1) * NC]
        p.v.inc(nv.tensor_copy(lnfmb[v], lnfm[v]))
    fmb_v = p.v.n

    def ln_stats(src_tile, sq_trash):
        for t in range(NT):
            src = src_tile(t)
            p.wait(ns, p.s)
            p.s.inc(ns.activation(sq_trash, src, AF.Square,
                                  accum_out=STATS[:, ST_T0 + t:ST_T0 + t + 1]))
            p.v.inc(nv.reduce_sum(STATS[:, ST_MU + t:ST_MU + t + 1], src, axis=AX.X))
        p.wait(nv, p.v)
        p.wait(nv, p.s)
        mu = STATS[:, ST_MU:ST_MU + NT]
        var = STATS[:, ST_T0:ST_T0 + NT]
        tmp = STATS[:, ST_T1:ST_T1 + NT]
        neg = STATS[:, ST_NEG:ST_NEG + NT]
        rstd = STATS[:, ST_RSTD:ST_RSTD + NT]
        p.v.inc(nv.tensor_scalar_mul(mu, mu, 1.0 / D))
        p.wait(nv, p.v)
        p.v.inc(nv.tensor_scalar_mul(neg, mu, -1.0))
        p.v.inc(nv.tensor_mul(tmp, mu, mu))
        p.wait(nv, p.v)
        p.v.inc(nv.tensor_scalar_mul(var, var, 1.0 / D))
        p.wait(nv, p.v)
        p.v.inc(nv.tensor_sub(var, var, tmp))
        p.wait(nv, p.v)
        p.v.inc(nv.tensor_scalar_add(tmp, var, EPS))
        p.wait(ns, p.v)
        p.s.inc(ns.activation(tmp, tmp, AF.Sqrt))
        p.wait(nv, p.s)
        p.v.inc(nv.reciprocal(rstd, tmp))
        p.wait(nv, p.v)

    def tmaj_to_fmaj(src_tile_bf, dstT, tag):
        for t in range(NT):
            half = t % 2
            stg = ROWSB[:, 0:1024] if half == 0 else ROWSB[:, 1024:2048]
            src_tile_bf(t, stg)
            p.wait(nt_, p.v)
            p.wait(nt_, p.s)
            dstp = PT[:, half * 1024:(half + 1) * 1024]
            for c in range(NC):
                p.t.inc(nt_.transpose(dstp[:, c * 128:(c + 1) * 128],
                                      stg[:, c * 128:(c + 1) * 128], ident[:]))
            p.wait(ns, p.t)
            dstv = dstT[:].rearrange("p (c tt q) -> p c tt q", c=NC, tt=NT)[:, :, t, :]
            srcv = dstp.rearrange("p (c q) -> p c q", c=NC)
            p.s.inc(ns.copy(dstv, srcv))

    def load_w_chunks(dst, src_hbm, ncol, ch):
        ch.inc(ng.dma_start(dst[:], src_hbm), 16)

    def bias_row_mms(b_fm_bf, w_sl, ncol, pbank_off, nchunk=NC):
        p.wait(nt_, p.v)
        p.wait(nt_, p.s)
        nblk = (ncol + 511) // 512
        for c in range(nchunk):
            for b in range(nblk):
                w = min(512, ncol - b * 512)
                p.t.inc(nt_.matmul(
                    PF[0:1, pbank_off + b * 512:pbank_off + b * 512 + w],
                    b_fm_bf[:, c:c + 1],
                    w_sl[:, c * ncol + b * 512:c * ncol + b * 512 + w],
                    start=(c == 0), stop=(c == nchunk - 1)))

    def row_to_col(row_bf, col_dst_f32):
        p.wait(nt_, p.s)
        p.t.inc(nt_.matmul(PF[0:128, 5 * 512:5 * 512 + 1], row_bf,
                           ones_row[0:1, 0:1], start=True, stop=True))
        p.wait(ns, p.t)
        p.s.inc(ns.copy(col_dst_f32, PF[0:128, 5 * 512:5 * 512 + 1]))

    xhat2T = p.sb("xhat2T", A0, [128, NC * S], BF16)
    vball = p.sb("vball", A1, [128, NT * H * 65], BF16)
    attT = p.sb("attT", A2, [128, NC * S], BF16)
    vW = p.sb("vW", A2, [128, NC * D], BF16)
    qTr = p.sb("qTr", GP0, [128, S], BF16)
    kTr = p.sb("kTr", GP0 + 4 * KB, [128, S], BF16)
    wq_sl = p.sb("wq_sl", GP0 + 8 * KB, [128, NC * 128], BF16)
    wk_sl = p.sb("wk_sl", GP0 + 10 * KB, [128, NC * 128], BF16)
    pu2 = [p.sb("pu20", GP0 + 12 * KB, [128, 1024], BF16),
           p.sb("pu21", GP0 + 14 * KB, [128, 1024], BF16)]
    rsinv = p.sb("rsinv", GP0 + 10 * KB, [1, 512], F32)
    rsbc = p.sb("rsbc", GP0 + 8 * KB, [128, 512], F32)

    p.wait(ns, p.d, ld0)
    p.wait(nv, p.d, ld0)

    ln_stats(lambda t: R[:, t * D:(t + 1) * D],
             sq_trash=ROWS[:, 0:1024])
    var1 = STATS[:, ST_T0:ST_T0 + NT]
    t1 = STATS[:, ST_T1:ST_T1 + NT]
    rp = STATS[:, ST_RP:ST_RP + NT]
    rstd1 = STATS[:, ST_RSTD:ST_RSTD + NT]
    p.v.inc(nv.tensor_mul(t1, var1, rstd1))
    p.wait(nv, p.v)
    p.v.inc(nv.tensor_mul(t1, t1, rstd1))
    p.wait(nv, p.v)
    p.v.inc(nv.tensor_scalar_add(t1, t1, EPS))
    p.wait(ns, p.v)
    p.s.inc(ns.activation(t1, t1, AF.Sqrt))
    p.wait(nv, p.s)
    p.v.inc(nv.reciprocal(t1, t1))
    p.wait(nv, p.v)
    p.v.inc(nv.tensor_mul(rp, rstd1, t1))
    p.wait(nv, p.v)

    def mk_xhat2(t, stg):
        p.wait(nv, p.t)
        p.v.inc(nv.tensor_scalar(stg, R[:, t * D:(t + 1) * D],
                                 STATS[:, ST_NEG + t:ST_NEG + t + 1],
                                 STATS[:, ST_RP + t:ST_RP + t + 1],
                                 ALU.add, ALU.mult))
    tmaj_to_fmaj(mk_xhat2, xhat2T, "xh2")
    xh2_s = p.s.n

    load_w_chunks(vW, aa["Wv"], D, p.w[0])
    p.wait(nt_, p.w[0])
    p.wait(nt_, p.v, fmb_v)
    bias_row_mms(lnfmb["lnv_b"], vW, D, 0)
    bias_t = p.t.n
    p.wait(nc.sync, p.s)
    p.d.inc(nc.sync.dma_start(ROWS[0:1, 0:D], aa["bv"]), 16)
    p.wait(nv, p.d)
    p.wait(nv, p.t, bias_t)
    p.v.inc(nv.tensor_add(ROWSB[0:1, 0:D], PF[0:1, 0:D], ROWS[0:1, 0:D]))
    p.wait(nt_, p.v)
    for b_ in range(2):
        p.t.inc(nt_.matmul(bank(2 + b_), ones_row[:],
                           ROWSB[0:1, b_ * 512:(b_ + 1) * 512], start=True, stop=True))
    p.wait(ns, p.t)
    p.s.inc(ns.copy(ROWSB[:, 1024:2048], PF[:, 2 * 512:4 * 512]))
    vbias_s = p.s.n
    p.wait(nv, p.t, bias_t)
    for c in range(NC):
        p.v.inc(nv.tensor_scalar_mul(vW[:, c * D:(c + 1) * D],
                                     vW[:, c * D:(c + 1) * D],
                                     lnfm["lnv_s"][:, c:c + 1]))
    vw_scaled_v = p.v.n
    p.wait(nv, p.xs, xs_zero)
    p.v.inc(nv.memset(vball[:], 1.0))
    vb_ones_v = p.v.n
    vb_r = vball[:].rearrange("p (t h c) -> p t h c", t=NT, h=H)
    for t in range(NT):
        pb = 0 if (t % 2 == 0) else 2
        p.wait(nt_, p.s, max(xh2_s, vbias_s))
        p.wait(nt_, p.v, max(vw_scaled_v, vb_ones_v))
        if t >= 2:
            p.wait(nt_, p.v)
        for c in range(NC):
            for b_ in range(2):
                p.t.inc(nt_.matmul(
                    bank(pb + b_),
                    xhat2T[:, c * S + t * 128:c * S + (t + 1) * 128],
                    vW[:, c * D + b_ * 512:c * D + (b_ + 1) * 512],
                    start=(c == 0), stop=(c == NC - 1)))
        p.wait(nv, p.t)
        p.wait(nv, p.s, vbias_s)
        p.v.inc(nv.tensor_add(
            vb_r[:, t, :, 0:64],
            bank2(pb).rearrange("p (h c) -> p h c", h=H),
            ROWSB[:, 1024:2048].rearrange("p (h c) -> p h c", h=H)))
    v_done_v = p.v.n

    p.wait(nc.sync, p.v)
    p.d.inc(nc.sync.dma_start(ROWS[0:1, 0:D], aa["bq"]), 16)
    p.d.inc(nc.sync.dma_start(ROWS[0:1, 1024:1024 + D], aa["bk"]), 16)
    bqk_d = p.d.n

    qb_col = FMV[:, 64:65]
    kb_col = FMV[:, 65:66]
    wsl_t = 0
    for hp in range(NC):
        conv_step(7)
        p.wait(ng, p.v, v_done_v)
        p.wait(ng, p.t, wsl_t)
        p.wait(ng, p.v)
        p.w[1].inc(ng.dma_start(
            wq_sl[:], aa["Wq"][:, hp * 1024:(hp + 1) * 1024]), 16)
        p.w[2].inc(ng.dma_start(
            wk_sl[:], aa["Wk"][:, hp * 1024:(hp + 1) * 1024]), 16)
        p.wait(nt_, p.w[1])
        p.wait(nt_, p.w[2])
        p.wait(nt_, p.v, fmb_v)
        bias_row_mms(lnfmb["lnq_b"], wq_sl, 128, 4 * 512)
        p.wait(nv, p.t)
        p.wait(nv, p.d, bqk_d)
        p.v.inc(nv.tensor_add(ROWSB[0:1, 0:128], PF[0:1, 4 * 512:4 * 512 + 128],
                              ROWS[0:1, hp * 128:(hp + 1) * 128]))
        p.wait(nt_, p.v)
        row_to_col(ROWSB[0:1, 0:128], qb_col)
        bias_row_mms(lnfmb["lnk_b"], wk_sl, 128, 4 * 512)
        p.wait(nv, p.t)
        p.v.inc(nv.tensor_add(ROWSB[0:1, 128:256], PF[0:1, 4 * 512:4 * 512 + 128],
                              ROWS[0:1, 1024 + hp * 128:1024 + (hp + 1) * 128]))
        p.wait(nt_, p.v)
        row_to_col(ROWSB[0:1, 128:256], kb_col)
        rowcol_s = p.s.n
        for c in range(NC):
            p.v.inc(nv.tensor_scalar_mul(wq_sl[:, c * 128:(c + 1) * 128],
                                         wq_sl[:, c * 128:(c + 1) * 128],
                                         lnfm["lnq_s"][:, c:c + 1]))
            p.v.inc(nv.tensor_scalar_mul(wk_sl[:, c * 128:(c + 1) * 128],
                                         wk_sl[:, c * 128:(c + 1) * 128],
                                         lnfm["lnk_s"][:, c:c + 1]))
        wsl_v = p.v.n
        for (wsl, dst, bcol) in ((wq_sl, qTr, qb_col), (wk_sl, kTr, kb_col)):
            p.wait(nt_, p.v, wsl_v)
            p.wait(nt_, p.s)
            for qb in range(4):
                for c in range(NC):
                    p.t.inc(nt_.matmul(
                        bank(qb), wsl[:, c * 128:(c + 1) * 128],
                        xhat2T[:, c * S + qb * 512:c * S + (qb + 1) * 512],
                        start=(c == 0), stop=(c == NC - 1)))
            p.wait(ns, p.t)
            p.wait(ns, p.s, rowcol_s)
            for qb in range(4):
                p.s.inc(ns.activation(dst[:, qb * 512:(qb + 1) * 512], bank(qb),
                                      AF.Identity, bias=bcol, scale=1.0))
        proj_s = p.s.n
        wsl_t = p.t.n

        exp_ms = {0: proj_s, 1: proj_s}
        p.wait(nt_, p.s, proj_s)
        for qb in range(4):
            def av_mm(kt):
                pp_ = kt % 2
                p.wait(nt_, p.s, exp_ms[pp_])
                p.t.inc(nt_.matmul(
                    PF[0:65, 0:512],
                    vball[:, kt * H * 65 + 2 * hp * 65:kt * H * 65 + (2 * hp + 1) * 65],
                    pu2[pp_][:, 0:512], start=(kt == 0), stop=(kt == NT - 1)))
                p.t.inc(nt_.matmul(
                    PF[0:65, 512:1024],
                    vball[:, kt * H * 65 + (2 * hp + 1) * 65:kt * H * 65 + (2 * hp + 2) * 65],
                    pu2[pp_][:, 512:1024], start=(kt == 0), stop=(kt == NT - 1)))
            for kt in range(NT):
                pp_ = kt % 2
                scb = 2 + 2 * pp_
                p.wait(nt_, p.s, exp_ms[pp_])
                p.t.inc(nt_.matmul(
                    bank(scb), kTr[0:64, kt * 128:(kt + 1) * 128],
                    qTr[0:64, qb * 512:(qb + 1) * 512],
                    start=True, stop=True, tile_position=(0, 0)))
                p.t.inc(nt_.matmul(
                    bank(scb + 1), kTr[64:128, kt * 128:(kt + 1) * 128],
                    qTr[64:128, qb * 512:(qb + 1) * 512],
                    start=True, stop=True, tile_position=(64, 0)))
                sc_t = p.t.n
                if kt == NT - 1 and qb == 0:
                    p.wait(nv, p.t, sc_t)
                    p.v.inc(nv.tensor_add(PF[:, scb * 512:scb * 512 + 1],
                                          PF[:, scb * 512:scb * 512 + 1], colmask[:]))
                    p.v.inc(nv.tensor_add(PF[:, (scb + 1) * 512:(scb + 1) * 512 + 1],
                                          PF[:, (scb + 1) * 512:(scb + 1) * 512 + 1],
                                          colmask[:]))
                    p.wait(ns, p.v)
                p.wait(ns, p.t, sc_t)
                exp_ms[pp_] = p.s.inc(
                    ns.activation(pu2[pp_][:], PF[:, scb * 512:(scb + 2) * 512],
                                  AF.Exp, scale=1.0 / 32.0))
                if kt >= 1:
                    av_mm(kt - 1)
            av_mm(NT - 1)
            av_t = p.t.n
            for hh in range(2):
                base = hh * 64
                p.wait(nv, p.t, av_t)
                p.v.inc(nv.reciprocal(rsinv[:], PF[64:65, hh * 512:hh * 512 + 512]))
                p.wait(nt_, p.v)
                p.wait(nt_, p.s, exp_ms[0])
                p.t.inc(nt_.matmul(bank(2), ones_row_f32[:], rsinv[:],
                                   start=True, stop=True))
                p.wait(ns, p.t)
                p.wait(ns, p.v)
                exp_ms[0] = p.s.inc(ns.copy(rsbc[:], bank(2)))
                p.wait(nv, p.s)
                p.v.inc(nv.tensor_mul(
                    attT[base:base + 64, hp * S + qb * 512:hp * S + (qb + 1) * 512],
                    PF[0:64, hh * 512:hh * 512 + 512], rsbc[0:64, :]))
            p.wait(nt_, p.v)

    woW = p.sb("woW", GP0, [128, NC * D], BF16)
    p.wait(ng, p.s)
    p.wait(ng, p.v)
    p.wait(ng, p.t)
    load_w_chunks(woW, aa["Wo"], D, p.w[0])
    p.wait(nc.sync, p.v)
    p.d.inc(nc.sync.dma_start(ROWS[0:1, 0:D], aa["bo"]), 16)
    p.wait(nv, p.d)
    p.v.inc(nv.tensor_copy(ROWSB[0:1, 0:D], ROWS[0:1, 0:D]))
    bo_v = p.v.n
    p.wait(nt_, p.w[0])
    h_bf = p.sb("h_bf", A1, [128, NT * D], BF16)
    for t in range(NT):
        pb = 0 if (t % 2 == 0) else 2
        p.wait(nt_, p.v)
        p.wait(nt_, p.v, bo_v)
        for c in range(NC):
            for b_ in range(2):
                p.t.inc(nt_.matmul(
                    bank(pb + b_),
                    attT[:, c * S + t * 128:c * S + (t + 1) * 128],
                    woW[:, c * D + b_ * 512:c * D + (b_ + 1) * 512],
                    start=(c == 0), stop=False))
        for b_ in range(2):
            p.t.inc(nt_.matmul(bank(pb + b_), ones_row[:],
                               ROWSB[0:1, b_ * 512:(b_ + 1) * 512],
                               start=False, stop=True))
        wo_t = p.t.n
        xa = ROWS[:, 1024:2048][:, 0:D]
        p.wait(nv, p.v)
        p.wait(nv, p.s)
        p.v.inc(nv.tensor_scalar(xa, R[:, t * D:(t + 1) * D],
                                 STATS[:, ST_NEG + t:ST_NEG + t + 1],
                                 STATS[:, ST_RSTD + t:ST_RSTD + t + 1],
                                 ALU.add, ALU.mult))
        p.wait(nv, p.t, wo_t)
        p.wait(nv, p.v)
        p.v.inc(nv.tensor_add(xa, bank2(pb), xa))
        p.wait(nv, p.v)
        p.v.inc(nv.tensor_add(R[:, t * D:(t + 1) * D],
                              R[:, t * D:(t + 1) * D], xa))
        p.wait(ns, p.v)
        p.s.inc(ns.copy(h_bf[:, t * D:(t + 1) * D], xa))
    h_done_s = p.s.n
    h_done_v = p.v.n

    p.wait(nv, p.t)
    p.wait(ns, p.t)
    p.wait(nv, p.s, h_done_s)
    ln_stats(lambda t: h_bf[:, t * D:(t + 1) * D], sq_trash=ROWS[:, 0:1024])
    yT = xhat2T

    def mk_y(t, stg):
        p.wait(nv, p.t)
        p.v.inc(nv.tensor_scalar(stg, h_bf[:, t * D:(t + 1) * D],
                                 STATS[:, ST_NEG + t:ST_NEG + t + 1],
                                 STATS[:, ST_RSTD + t:ST_RSTD + t + 1],
                                 ALU.add, ALU.mult))
    tmaj_to_fmaj(mk_y, yT, "yT")
    yT_s = p.s.n

    xmT = p.sb("xmT", A0, [128, NC * S], BF16)
    pm2f = [p.sb(f"pm2f{i}", A0 + i * 2 * KB, [128, 512], F32) for i in (0, 1)]
    MEbuf = p.sb("MEbuf", A0, [E, S], F32)
    gstb = p.sb("gstb", A0, [128, 2048], BF16)
    xmB = p.sb("xmB", A1, [128, NT * D], BF16)
    ws = [p.sb(f"ws{i}", A2 + i * 8 * KB, [128, NC * 512], BF16) for i in range(4)]
    hi_stage = p.sb("hi_stage", GP0, [128, 1024], BF16)
    loT = p.sb("loT", GP0 + 2 * KB, [128, 1024], BF16)
    rankm1 = p.sb("rankm1", GP0 + 4 * KB, [E, S], BF16)
    rhl = p.sb("rhl", GP0 + 4 * KB, [128, 1024], BF16)
    wTg = [p.sb("wTg0", 78 * KB, [E, S], BF16),
           p.sb("wTg1", 70 * KB, [E, S], BF16)]
    stg_e = [p.sb(f"stg_e{i}", GP0 + i * 4 * KB, [128, 2 * D], BF16) for i in (0, 1)]
    xsTe = [p.sb(f"xsTe{i}", GP0 + 8 * KB + i * 4 * KB, [128, NC * CAP], BF16)
            for i in (0, 1)]
    ysst = [ROWSB[:, 0:D], ROWSB[:, D:2 * D]]
    DESTI = p.sb("DESTI", GP0 + 16 * KB, [128, 8], mybir.dt.int32)

    w1q = p.sb("w1q", GP0, [128, NC * 512], BF16)
    w2q = p.sb("w2q", GP0 + 8 * KB, [128, 4 * D], BF16)
    ghT = p.sb("ghT", A2, [128, 4 * S], BF16)
    hpre = p.sb("hpre", A2 + 16 * KB, [128, 2048], F32)
    gel1 = p.sb("gel1", A2 + 24 * KB, [128, 2048], F32)

    rst = {"evac": 0, "rhl_v": 0, "wT_s": 0, "lg_t": 0}

    def rt_prep():
        p.wait(nc.sync, p.s)
        p.wait(nc.sync, p.v)
        for g in (0, 1):
            p.d.inc(nc.sync.dma_start(
                ROWS[:, 256 + g * 256:256 + (g + 1) * 256]
                .rearrange("p (c e) -> p c e", c=NC),
                ff[f"r{g}"].rearrange("(c p) e -> p c e", p=128)), 16)
        rt_d = p.d.n
        p.wait(nv, p.d, rt_d)
        p.wait(nv, p.t)
        p.wait(nv, p.s)
        for g in (0, 1):
            rf = ROWS[:, 256 + g * 256:256 + (g + 1) * 256]
            p.v.inc(nv.tensor_copy(rhl[:, g * 256:(g + 1) * 256], rf))
        p.wait(nv, p.v)
        for g in (0, 1):
            rf = ROWS[:, 256 + g * 256:256 + (g + 1) * 256]
            rhi = rhl[:, g * 256:(g + 1) * 256]
            p.v.inc(nv.tensor_sub(rhl[:, 512 + g * 256:512 + (g + 1) * 256],
                                  rf, rhi))
        rst["rhl_v"] = p.v.n
        rst["evac"] = p.s.n

    def rt_stats(t, trash):
        src = R[:, t * D:(t + 1) * D]
        p.wait(ns, p.s)
        p.wait(ns, p.v)
        p.s.inc(ns.activation(trash, src, AF.Square,
                              accum_out=STATS[:, ST_T0 + t:ST_T0 + t + 1]))
        p.wait(nv, p.v)
        p.v.inc(nv.reduce_sum(STATS[:, ST_MU + t:ST_MU + t + 1], src, axis=AX.X))

    def rt_fin(t):
        mu = STATS[:, ST_MU + t:ST_MU + t + 1]
        var = STATS[:, ST_T0 + t:ST_T0 + t + 1]
        tmp = STATS[:, ST_T1 + t:ST_T1 + t + 1]
        neg = STATS[:, ST_NEG + t:ST_NEG + t + 1]
        rstd = STATS[:, ST_RSTD + t:ST_RSTD + t + 1]
        p.wait(nv, p.v)
        p.wait(nv, p.s)
        p.v.inc(nv.tensor_scalar(neg, mu, -1.0 / D, None, ALU.mult))
        p.v.inc(nv.tensor_scalar(tmp, mu, 1.0 / D, None, ALU.mult))
        p.wait(nv, p.v)
        p.v.inc(nv.tensor_mul(tmp, tmp, tmp))
        p.v.inc(nv.tensor_scalar(var, var, 1.0 / D, EPS, ALU.mult, ALU.add))
        p.wait(nv, p.v)
        p.v.inc(nv.tensor_sub(var, var, tmp))
        p.wait(ns, p.v)
        p.s.inc(ns.activation(var, var, AF.Sqrt))
        p.wait(nv, p.s)
        p.v.inc(nv.reciprocal(rstd, var))

    def rt_fin_all():
        p.wait(nv, p.v)
        p.wait(nv, p.s)
        mu = STATS[:, ST_MU:ST_MU + NT]
        var = STATS[:, ST_T0:ST_T0 + NT]
        tmp = STATS[:, ST_T1:ST_T1 + NT]
        neg = STATS[:, ST_NEG:ST_NEG + NT]
        rstd = STATS[:, ST_RSTD:ST_RSTD + NT]
        p.v.inc(nv.tensor_scalar_mul(mu, mu, 1.0 / D))
        p.wait(nv, p.v)
        p.v.inc(nv.tensor_scalar_mul(neg, mu, -1.0))
        p.v.inc(nv.tensor_mul(tmp, mu, mu))
        p.wait(nv, p.v)
        p.v.inc(nv.tensor_scalar_mul(var, var, 1.0 / D))
        p.wait(nv, p.v)
        p.v.inc(nv.tensor_sub(var, var, tmp))
        p.wait(nv, p.v)
        p.v.inc(nv.tensor_scalar_add(tmp, var, EPS))
        p.wait(ns, p.v)
        p.s.inc(ns.activation(tmp, tmp, AF.Sqrt))
        p.wait(nv, p.s)
        p.v.inc(nv.reciprocal(rstd, tmp))
        p.wait(nv, p.v)

    def rt_chain(t):
        xm = ROWS[:, 1024:2048]
        p.wait(nv, p.v)
        p.wait(nv, p.t)
        p.v.inc(nv.tensor_scalar(xm, R[:, t * D:(t + 1) * D],
                                 STATS[:, ST_NEG + t:ST_NEG + t + 1],
                                 STATS[:, ST_RSTD + t:ST_RSTD + t + 1],
                                 ALU.add, ALU.mult))
        p.wait(nv, p.v)
        hi = xmB[:, t * D:(t + 1) * D]
        p.v.inc(nv.tensor_copy(hi, xm))
        p.wait(nt_, p.v)
        p.wait(nt_, p.s, rst["evac"])
        for c in range(NC):
            p.t.inc(nt_.transpose(PT[:, c * 128:(c + 1) * 128],
                                  hi[:, c * 128:(c + 1) * 128], ident[:]))
        hi_t = p.t.n
        p.wait(ns, p.t, hi_t)
        dstv = xmT[:].rearrange("p (c tt q) -> p c tt q", c=NC, tt=NT)[:, :, t, :]
        p.s.inc(ns.copy(dstv, PT[:, 0:1024].rearrange("p (c q) -> p c q", c=NC)))
        hi_evac = p.s.n
        p.wait(nv, p.t, hi_t)
        p.v.inc(nv.tensor_sub(hi_stage[:], xm, hi))
        p.wait(nt_, p.v)
        for c in range(NC):
            p.t.inc(nt_.transpose(PT[:, 1024 + c * 128:1024 + (c + 1) * 128],
                                  hi_stage[:, c * 128:(c + 1) * 128], ident[:]))
        lo_t = p.t.n
        p.wait(ns, p.t, lo_t)
        p.s.inc(ns.copy(loT[:].rearrange("p (c q) -> p c q", c=NC),
                        PT[:, 1024:2048].rearrange("p (c q) -> p c q", c=NC)))
        rst["evac"] = p.s.n
        p.wait(nt_, p.s, rst["evac"])
        p.wait(nt_, p.v, rst["rhl_v"])
        xtile = xmT[:].rearrange("p (c tt q) -> p c tt q", c=NC, tt=NT)
        for g in (0, 1):
            dst = PF[:, g * 512 + t * E:g * 512 + (t + 1) * E]
            rhi = rhl[:, g * 256:(g + 1) * 256]
            rlo = rhl[:, 512 + g * 256:512 + (g + 1) * 256]
            first = True
            for c in range(NC):
                lhs = xtile[:, c, t, :]
                p.t.inc(nt_.matmul(dst, lhs,
                                   rhi[:, c * E:(c + 1) * E],
                                   start=first, stop=False))
                first = False
            for c in range(NC):
                lhs = xtile[:, c, t, :]
                p.t.inc(nt_.matmul(dst, lhs,
                                   rlo[:, c * E:(c + 1) * E],
                                   start=False, stop=False))
            for c in range(NC):
                p.t.inc(nt_.matmul(dst,
                                   loT[:, c * 128:(c + 1) * 128],
                                   rhi[:, c * E:(c + 1) * E],
                                   start=False, stop=(c == NC - 1)))
        rst["lg_t"] = p.t.n
    def rt_top2():
        lgp = [PF[:, 0:512], PF[:, 512:1024]]
        mx = [FMV[:, g * 48:g * 48 + 16] for g in (0, 1)]
        sume = [FMV[:, g * 48 + 16:g * 48 + 32] for g in (0, 1)]
        rec = [FMV[:, g * 48 + 32:g * 48 + 48] for g in (0, 1)]
        m2c = sume
        ptmp = [ROWS[:, g * 512:(g + 1) * 512] for g in (0, 1)]
        msk = [ROWS[:, 1024 + g * 512:1536 + g * 512] for g in (0, 1)]
        pmb = [pm2f[g][:] for g in (0, 1)]
        p.wait(nv, p.t, rst["lg_t"])
        p.wait(nv, p.v)
        p.wait(nv, p.s)
        for g in (0, 1):
            p.v.inc(nv.reduce_max(mx[g], lgp[g].rearrange("p (t e) -> p t e", t=NT),
                                  axis=AX.X, negate=True))
        p.wait(ns, p.v)
        for g in (0, 1):
            for t in range(NT):
                p.s.inc(ns.activation(ptmp[g][:, t * E:(t + 1) * E],
                                      lgp[g][:, t * E:(t + 1) * E], AF.Exp,
                                      bias=mx[g][:, t:t + 1],
                                      accum_out=sume[g][:, t:t + 1]))
        p.wait(nv, p.s)
        for g in (0, 1):
            p.v.inc(nv.reciprocal(rec[g], sume[g]))
            p.v.inc(nv.tensor_scalar(msk[g], ptmp[g], 1.0, None, ALU.is_ge))
        p.wait(nv, p.v)
        for g in (0, 1):
            for t in range(NT):
                p.v.inc(nv.tensor_scalar_mul(ptmp[g][:, t * E:(t + 1) * E],
                                             ptmp[g][:, t * E:(t + 1) * E],
                                             rec[g][:, t:t + 1]))
        p.wait(nv, p.v)
        for g in (0, 1):
            p.v.inc(nv.scalar_tensor_tensor(pmb[g], msk[g], -3.0, ptmp[g],
                                            ALU.mult, ALU.add))
        p.wait(nv, p.v)
        for g in (0, 1):
            p.v.inc(nv.reduce_max(m2c[g], pmb[g].rearrange("p (t e) -> p t e", t=NT),
                                  axis=AX.X))
            p.v.inc(nv.tensor_mul(ROWSB[:, g * 1024:g * 1024 + 512],
                                  msk[g], ptmp[g]))
        p.wait(nv, p.v)
        for g in (0, 1):
            for t in range(NT):
                p.v.inc(nv.tensor_scalar(pmb[g][:, t * E:(t + 1) * E],
                                         ptmp[g][:, t * E:(t + 1) * E],
                                         m2c[g][:, t:t + 1], None, ALU.is_ge))
        p.wait(nv, p.v)
        for g in (0, 1):
            p.v.inc(nv.tensor_sub(pmb[g], pmb[g], msk[g]))
        p.wait(nv, p.v)
        for g in (0, 1):
            p.v.inc(nv.tensor_mul(ROWSB[:, g * 1024 + 512:g * 1024 + 1024],
                                  pmb[g], ptmp[g]))

    def rt_wT_all():
        p.wait(nv, p.v)
        for g in (0, 1):
            p.v.inc(nv.tensor_add(hi_stage[:, g * 512:(g + 1) * 512],
                                  ROWSB[:, g * 1024:g * 1024 + 512],
                                  ROWSB[:, g * 1024 + 512:g * 1024 + 1024]))
        p.wait(nt_, p.v)
        p.wait(nt_, p.s)
        for g in (0, 1):
            for t in range(NT):
                p.t.inc(nt_.transpose(
                    PT[g * E:(g + 1) * E, t * 128:(t + 1) * 128],
                    hi_stage[:, g * 512 + t * E:g * 512 + (t + 1) * E], ident[:]))
        p.wait(ns, p.t)
        for g in (0, 1):
            p.s.inc(ns.copy(wTg[g][:], PT[g * E:(g + 1) * E, 0:NT * 128]))
        rst["evac"] = p.s.n
        rst["wT_s"] = p.s.n

    p.wait(nc.sync, p.v)
    p.wait(nc.sync, p.s)
    p.d.inc(nc.sync.dma_start(ROWS[0:1, 0:FF], aa["b1"]), 16)
    p.d.inc(nc.sync.dma_start(ROWS[32:33, 0:D], aa["b2"]), 16)
    b12_d = p.d.n
    b1c = FMV[:, 66:70]
    w2_add_v = 0
    for qh in range(4):
        conv_step(6)
        p.wait(ng, p.t)
        p.w[1].inc(ng.dma_start(
            w1q[:], aa["W1"][:, qh * 4096:(qh + 1) * 4096]), 16)
        p.w[2].inc(ng.dma_start(
            w2q[:], aa["W2"][:, qh * 4096:(qh + 1) * 4096]), 16)
        p.wait(nt_, p.w[1])
        p.wait(nt_, p.w[2])
        p.wait(nt_, p.v, fmb_v)
        p.wait(nt_, p.v)
        bias_row_mms(lnfmb["lnm_b"], w1q, 512, 4 * 512)
        p.wait(nv, p.t)
        p.wait(nv, p.d, b12_d)
        p.v.inc(nv.tensor_add(ROWSB[0:1, 0:512], PF[0:1, 4 * 512:5 * 512],
                              ROWS[0:1, qh * 512:(qh + 1) * 512]))
        p.wait(nt_, p.v)
        for cc in range(4):
            row_to_col(ROWSB[0:1, cc * 128:(cc + 1) * 128], b1c[:, cc:cc + 1])
        b1c_s = p.s.n
        for c in range(NC):
            p.v.inc(nv.tensor_scalar_mul(w1q[:, c * 512:(c + 1) * 512],
                                         w1q[:, c * 512:(c + 1) * 512],
                                         lnfm["lnm_s"][:, c:c + 1]))
        w1s_v = p.v.n
        hpre_ms = 0
        for hc in range(4):
            p.wait(nt_, p.v, w1s_v)
            p.wait(nt_, p.s, max(hpre_ms, b1c_s))
            for tb in range(4):
                for c in range(NC):
                    p.t.inc(nt_.matmul(
                        bank(tb), w1q[:, c * 512 + hc * 128:c * 512 + (hc + 1) * 128],
                        yT[:, c * S + tb * 512:c * S + (tb + 1) * 512],
                        start=(c == 0), stop=(c == NC - 1)))
            mm_t = p.t.n
            pfull = PF[:, 0:2048]
            p.wait(ns, p.t, mm_t)
            p.wait(ns, p.s, b1c_s)
            p.wait(ns, p.v)
            hpre_ms = p.s.inc(
                ns.activation(hpre[:], pfull, AF.Identity, bias=b1c[:, hc:hc + 1]))
            p.wait(nv, p.s)
            p.v.inc(nv.tensor_mul(gel1[:], hpre[:], hpre[:]))
            p.wait(nv, p.v)
            p.v.inc(nv.tensor_scalar(gel1[:], gel1[:], GELU_C, 1.0, ALU.mult, ALU.add))
            p.wait(nv, p.v)
            p.v.inc(nv.tensor_mul(gel1[:], gel1[:], hpre[:]))
            p.wait(ns, p.v)
            p.s.inc(ns.activation(gel1[:], gel1[:], AF.Tanh, scale=GELU_S))
            p.wait(nv, p.s)
            p.v.inc(nv.tensor_scalar(gel1[:], gel1[:], 1.0, 0.5, ALU.add, ALU.mult))
            p.wait(nv, p.v)
            p.v.inc(nv.tensor_mul(ghT[:, hc * S:(hc + 1) * S], gel1[:], hpre[:]))
        gh_v = p.v.n
        p.wait(nt_, p.v, gh_v)
        if qh == 0:
            p.wait(nv, p.d, b12_d)
            p.v.inc(nv.tensor_copy(ROWSB[0:1, 1024:1024 + D], ROWS[32:33, 0:D]))
            p.wait(nt_, p.v)
        rt_on = (qh == 3 and stage != "A")
        if rt_on:
            rt_prep()
        for t in range(NT):
            if w2_add_v:
                p.wait(nt_, p.v, w2_add_v)
            for hc in range(4):
                for b_ in range(2):
                    p.t.inc(nt_.matmul(
                        bank(4 + b_),
                        ghT[:, hc * S + t * 128:hc * S + (t + 1) * 128],
                        w2q[:, hc * D + b_ * 512:hc * D + (b_ + 1) * 512],
                        start=(hc == 0), stop=(hc == 3 and qh != 0)))
            if qh == 0:
                for b_ in range(2):
                    p.t.inc(nt_.matmul(bank(4 + b_), ones_row[:],
                                       ROWSB[0:1, 1024 + b_ * 512:1024 + (b_ + 1) * 512],
                                       start=False, stop=True))
            w2mm_t = p.t.n
            p.wait(nv, p.t, w2mm_t)
            p.v.inc(nv.tensor_add(R[:, t * D:(t + 1) * D],
                                  R[:, t * D:(t + 1) * D], bank2(4)))
            w2_add_v = p.v.n
            if rt_on:
                rt_stats(t, hpre[:, 0:1024])
                if t >= 1:
                    rt_fin(t - 1)
                    rt_chain(t - 1)
        if rt_on:
            rt_fin(NT - 1)
            rt_chain(NT - 1)

    if stage == "A":
        _emit_out(p, R, out)
        p.stack.close()
        return nc




    def emit_moe(tag, pre=False):
        for eng in (nv, ns, nt_, ng, nc.sync):
            p.wait(eng, p.v)
            p.wait(eng, p.s)
            p.wait(eng, p.t)

        conv_step(len(conv_work))
        if not pre:
            for t in range(NT):
                rt_stats(t, ROWS[:, 0:1024])
            rt_fin_all()
            rt_prep()
            for t in range(NT):
                rt_chain(t)
        rt_top2()
        rt_wT_all()
        wT_s = rst["wT_s"]

        def scatter_closures(g):
            gb = 100 + g * 96
            work = []
            for t in range(NT):
                for si in range(2):
                    def go(t=t, si=si, g=g, gb=gb):
                        dcol = ROWS[:, gb + si * 16 + t:gb + si * 16 + t + 1]
                        di = DESTI[:, si:si + 1]
                        p.wait(nv, p.ga[0])
                        p.wait(nv, p.ga[1])
                        p.wait(nv, p.xs)
                        p.v.inc(nv.tensor_copy(di, dcol))
                        p.wait(ng, p.v)
                        p.wait(ng, p.xs, xs_zero)
                        p.xs.inc(ng.indirect_dma_start(
                            xs_d[g], bass.IndirectOffsetOnAxis(di, 0),
                            xmB[:, t * D:(t + 1) * D], None), 16)
                    work.append(go)
            return work

        def gather_closures(g, ys_total):
            gb = 100 + g * 96
            work = []
            for t in range(NT):
                for si in range(2):
                    def go(t=t, si=si, g=g, gb=gb, ys_total=ys_total):
                        gpar = (t * 2 + si) % 2
                        gst = gstb[:, gpar * 1024:(gpar + 1) * 1024]
                        dcol = ROWS[:, gb + si * 16 + t:gb + si * 16 + t + 1]
                        wcol = ROWS[:, gb + 32 + si * 16 + t:gb + 33 + si * 16 + t]
                        di = DESTI[:, 4 + si:5 + si]
                        p.wait(nv, p.ga[0])
                        p.wait(nv, p.ga[1])
                        p.v.inc(nv.tensor_copy(di, dcol))
                        p.wait(ng, p.v)
                        p.wait(ng, p.ys, ys_total)
                        p.ga[gpar].inc(ng.indirect_dma_start(
                            gst, None, ys_d[g],
                            bass.IndirectOffsetOnAxis(di, 0)), 16)
                        p.wait(nv, p.ga[gpar])
                        p.v.inc(nv.scalar_tensor_tensor(
                            R[:, t * D:(t + 1) * D], gst, wcol,
                            R[:, t * D:(t + 1) * D], ALU.mult, ALU.add))
                    work.append(go)
            return work

        for g in (0, 1):
            gb = 100 + g * 96
            p.wait(nv, p.s, wT_s)
            p.wait(nv, p.t)
            me = MEbuf[:]
            p.v.inc(nv.tensor_scalar(me, wTg[g][:], 0.0, None, ALU.is_gt))
            p.wait(nv, p.v)
            p.v.inc(nv.tensor_tensor_scan(me, me, me, 0.0, ALU.add, ALU.bypass))
            p.wait(nv, p.v)
            p.v.inc(nv.tensor_scalar(rankm1[:], me, -1.0, None, ALU.add))
            rk_v = p.v.n
            p.wait(nt_, p.v, rk_v)
            p.wait(nt_, p.s)
            for t in range(NT):
                p.t.inc(nt_.transpose(PT[:, t * E:(t + 1) * E],
                                      rankm1[:, t * 128:(t + 1) * 128],
                                      ident[0:E, 0:E]))
            p.wait(ns, p.t)
            p.wait(ns, p.v)
            rkT_all = hi_stage[:, 0:NT * E]
            p.s.inc(ns.copy(rkT_all, PT[:, 0:NT * E]))
            rkT_s = p.s.n
            p.wait(nv, p.s, rkT_s)
            msks = [ROWS[:, 1024:1536], ROWS[:, 1536:2048]]
            tmps = [pm2f[0][:], pm2f[1][:]]
            wall = [ROWSB[:, g * 1024 + si * 512:g * 1024 + (si + 1) * 512]
                    for si in range(2)]
            d16 = [ROWS[:, gb + si * 16:gb + si * 16 + 16] for si in range(2)]
            r16 = [ROWS[:, 64 + si * 16:80 + si * 16] for si in range(2)]
            w16 = [ROWS[:, gb + 32 + si * 16:gb + 48 + si * 16] for si in range(2)]
            p.wait(nv, p.v)
            for si in range(2):
                p.v.inc(nv.tensor_scalar(msks[si], wall[si], 0.0, None, ALU.is_gt))
            p.wait(nv, p.v)
            for si in range(2):
                p.v.inc(nv.tensor_mul(tmps[si], msks[si], eiota16[:]))
            p.wait(nv, p.v)
            for si in range(2):
                p.v.inc(nv.reduce_sum(
                    d16[si], tmps[si].rearrange("p (t e) -> p t e", t=NT), axis=AX.X))
            p.wait(nv, p.v)
            for si in range(2):
                p.v.inc(nv.tensor_mul(tmps[si], msks[si], rkT_all))
            p.wait(nv, p.v)
            for si in range(2):
                p.v.inc(nv.reduce_sum(
                    r16[si], tmps[si].rearrange("p (t e) -> p t e", t=NT), axis=AX.X))
                p.v.inc(nv.tensor_scalar(d16[si], d16[si], float(CAP), None, ALU.mult))
            p.wait(nv, p.v)
            for si in range(2):
                p.v.inc(nv.tensor_add(d16[si], d16[si], r16[si]))
                p.v.inc(nv.reduce_sum(
                    w16[si], wall[si].rearrange("p (t e) -> p t e", t=NT), axis=AX.X))
            p.wait(nv, p.v)
            for si in range(2):
                p.v.inc(nv.tensor_scalar_mul(w16[si], w16[si], 0.5))
            p.wait(nv, p.v)
            if g == 0:
                for go in scatter_closures(0):
                    go()

        bg = []
        for g in (0, 1):
            sc_total = p.xs.n
            wgA, wuA, wdA = wcache[(g, 0)], wcache[(g, 1)], wcache[(g, 2)]
            slot_t = [0, 0, 0, 0]

            def load_slot(i, src3, e):
                p.wait(ng, p.t, slot_t[i])
                p.wait(ng, p.cv)
                p.w[i].inc(ng.dma_start(ws[i][:], src3[e]), 16)

            load_slot(0, wgA, 0)
            load_slot(2, wuA, 0)
            p.wait(nc.sync, p.s)
            p.wait(nc.sync, p.v)
            stg_free = [0, 0]

            def issue_stream(e):
                par_ = e % 2
                p.wait(nc.sync, p.xs, sc_total)
                p.wait(nc.sync, p.t, stg_free[par_])
                p.si[par_].inc(nc.sync.dma_start(
                    stg_e[par_][:].rearrange("p (a w) -> p a w", a=2),
                    xs_d[g].rearrange("(a p) w -> p a w", p=128)[:, e * 2:e * 2 + 2, :]), 16)

            issue_stream(0)
            if g == 0:
                bg = scatter_closures(1)
            for e in range(E):
                par = e % 2
                stg = stg_e[par]
                xsT = xsTe[par]
                hT = stg[:, 0:1024]
                gsl = ws[par]
                usl = ws[2 + par]
                sich = p.si[par]
                p.wait(nt_, sich)
                p.wait(nt_, p.s)
                for r in range(2):
                    for c in range(NC):
                        p.t.inc(nt_.transpose(
                            PT[:, r * 1024 + c * 128:r * 1024 + (c + 1) * 128],
                            stg[:, r * D + c * 128:r * D + (c + 1) * 128], ident[:]))
                tr_t = p.t.n
                p.wait(ns, p.t, tr_t)
                dstv = xsT[:].rearrange("p (c r q) -> p c r q", c=NC, r=2)
                p.s.inc(ns.copy(dstv,
                                PT[:].rearrange("p (r c q) -> p c r q", r=2, c=NC)))
                xsT_s = p.s.n
                if e + 1 < E:
                    issue_stream(e + 1)
                p.wait(nt_, p.w[par])
                p.wait(nt_, p.s, xsT_s)
                p.wait(nt_, p.v)
                for ht in range(4):
                    bk = PF[:, ht * 256:(ht + 1) * 256]
                    for c in range(NC):
                        p.t.inc(nt_.matmul(
                            bk, gsl[:, c * 512 + ht * 128:c * 512 + (ht + 1) * 128],
                            xsT[:, c * CAP:(c + 1) * CAP],
                            start=(c == 0), stop=(c == NC - 1)))
                gate_t = p.t.n
                slot_t[par] = gate_t
                load_slot(par, wdA, e)
                p.wait(nt_, p.w[2 + par])
                for ht in range(4):
                    bk = PF[:, 1024 + ht * 256:1024 + (ht + 1) * 256]
                    for c in range(NC):
                        p.t.inc(nt_.matmul(
                            bk, usl[:, c * 512 + ht * 128:c * 512 + (ht + 1) * 128],
                            xsT[:, c * CAP:(c + 1) * CAP],
                            start=(c == 0), stop=(c == NC - 1)))
                up_t = p.t.n
                slot_t[2 + par] = up_t
                if e + 1 < E:
                    load_slot(1 - par, wgA, e + 1)
                    load_slot(3 - par, wuA, e + 1)
                for ht in range(4):
                    p.wait(ns, p.t, gate_t)
                    sg = ROWS[:, 1024:1280]
                    p.wait(ns, p.v)
                    p.s.inc(ns.activation(sg, PF[:, ht * 256:(ht + 1) * 256], AF.Silu))
                    p.wait(nv, p.s)
                    p.wait(nv, p.t, up_t)
                    p.v.inc(nv.tensor_mul(hT[:, ht * 256:(ht + 1) * 256],
                                          PF[:, 1024 + ht * 256:1024 + (ht + 1) * 256],
                                          sg))
                hT_v = p.v.n
                p.wait(nt_, p.v, hT_v)
                p.wait(nt_, p.w[par])
                for r in range(2):
                    p.wait(nt_, p.s)
                    for hc in range(4):
                        for b_ in range(2):
                            p.t.inc(nt_.matmul(
                                bank(4 + b_),
                                hT[:, hc * 256 + r * 128:hc * 256 + (r + 1) * 128],
                                gsl[:, hc * D + b_ * 512:hc * D + (b_ + 1) * 512],
                                start=(hc == 0), stop=(hc == 3)))
                    dn_t = p.t.n
                    p.wait(ns, p.t, dn_t)
                    ysb = ysst[(e * 2 + r) % 2]
                    p.wait(ns, p.ys)
                    p.s.inc(ns.copy(ysb, bank2(4)))
                    p.wait(nc.sync, p.s)
                    p.ys.inc(nc.sync.dma_start(
                        ys_d[g].rearrange("(a p) w -> p a w", p=128)[:, e * 2 + r:e * 2 + r + 1, :],
                        ysb.rearrange("p (a w) -> p a w", a=1)), 16)
                slot_t[par] = p.t.n
                stg_free[par] = p.t.n
                for _ in range(2):
                    if bg:
                        bg.pop(0)()
            ys_total = p.ys.n
            if g == 0:
                for go in bg:
                    go()
                bg = gather_closures(0, ys_total)
            else:
                for go in bg:
                    go()
                for go in gather_closures(1, ys_total):
                    go()

    xbT = p.sb("xbT", A0, [128, NC * S], BF16)

    def emit_attnb():
        for eng in (nv, ns, nt_, ng, nc.sync):
            p.wait(eng, p.v)
            p.wait(eng, p.s)
            p.wait(eng, p.t)
        p.d.inc(nc.sync.dma_start(cos2[:], cdram["cos2"]), 16)
        p.d.inc(nc.sync.dma_start(sin2[:], cdram["sin2"]), 16)
        p.d.inc(nc.sync.dma_start(perm128[:], cdram["perm128"]), 16)
        pc_d = p.d.n
        ones_col = FMB[:, 39:40]
        p.v.inc(nv.memset(ones_col, 1.0))
        oc_v = p.v.n

        ln_stats(lambda t: R[:, t * D:(t + 1) * D], sq_trash=ROWS[:, 0:1024])

        def mk_xb(t, stg):
            p.wait(nv, p.t)
            p.v.inc(nv.tensor_scalar(stg, R[:, t * D:(t + 1) * D],
                                     STATS[:, ST_NEG + t:ST_NEG + t + 1],
                                     STATS[:, ST_RSTD + t:ST_RSTD + t + 1],
                                     ALU.add, ALU.mult))
        tmaj_to_fmaj(mk_xb, xbT, "xb")
        xb_s = p.s.n

        load_w_chunks(vW, ab["WvT"], D, p.w[0])
        p.v.inc(nv.memset(vball[:], 1.0))
        vb_v = p.v.n
        p.wait(nt_, p.w[0])
        vb_r = vball[:].rearrange("p (t h c) -> p t h c", t=NT, h=H)
        for t in range(NT):
            pb = 0 if (t % 2 == 0) else 2
            p.wait(nt_, p.s, xb_s)
            p.wait(nt_, p.v, vb_v)
            p.wait(nt_, p.s)
            for c in range(NC):
                for b_ in range(2):
                    p.t.inc(nt_.matmul(
                        bank(pb + b_),
                        xbT[:, c * S + t * 128:c * S + (t + 1) * 128],
                        vW[:, c * D + b_ * 512:c * D + (b_ + 1) * 512],
                        start=(c == 0), stop=(c == NC - 1)))
            p.wait(ns, p.t)
            p.s.inc(ns.copy(vb_r[:, t, :, 0:64],
                            bank2(pb).rearrange("p (h c) -> p h c", h=H)))
        v_done_s = p.s.n

        def rope_apply(buf, evac_s):
            p.wait(nt_, p.s, evac_s)
            p.wait(nt_, p.v)
            p.wait(nt_, p.d, pc_d)
            for nb in range(4):
                p.t.inc(nt_.matmul(bank(nb), perm128[:],
                                   buf[:, nb * 512:(nb + 1) * 512],
                                   start=True, stop=True))
            pm_t = p.t.n
            p.wait(nv, p.t, pm_t)
            p.v.inc(nv.tensor_mul(ROWSB[:, 0:2048], PF[:, 0:2048], sin2[:]))
            p.wait(nv, p.v)
            p.v.inc(nv.tensor_mul(buf[:], buf[:], cos2[:]))
            p.wait(nv, p.v)
            p.v.inc(nv.tensor_add(buf[:], buf[:], ROWSB[:, 0:2048]))
            p.wait(nv, p.v)

        wsl_tc = 0
        for hp in range(NC):
            conv_step(4)
            p.wait(ng, p.t, wsl_tc)
            p.wait(ng, p.v)
            p.w[1].inc(ng.dma_start(
                wq_sl[:], ab["WqT"][:, hp * 1024:(hp + 1) * 1024]), 16)
            p.w[2].inc(ng.dma_start(
                wk_sl[:], ab["WkT"][:, hp * 1024:(hp + 1) * 1024]), 16)
            p.wait(nt_, p.w[1])
            p.wait(nt_, p.w[2])
            for (wsl, dst) in ((wq_sl, qTr), (wk_sl, kTr)):
                p.wait(nt_, p.s)
                p.wait(nt_, p.v)
                for qb in range(4):
                    for c in range(NC):
                        p.t.inc(nt_.matmul(
                            bank(qb), wsl[:, c * 128:(c + 1) * 128],
                            xbT[:, c * S + qb * 512:c * S + (qb + 1) * 512],
                            start=(c == 0), stop=(c == NC - 1)))
                p.wait(ns, p.t)
                ev = 0
                for qb in range(4):
                    ev = p.s.inc(ns.copy(dst[:, qb * 512:(qb + 1) * 512], bank(qb)))
                rope_apply(dst, ev)
            wsl_tc = p.t.n
            for (buf, col0) in ((qTr, 0), (kTr, 2)):
                p.wait(ns, p.v)
                p.s.inc(ns.activation(ROWSB[:, 0:2048], buf[:], AF.Square))
                p.wait(nt_, p.s)
                p.wait(nt_, p.v, oc_v)
                for hh in range(2):
                    p.wait(nt_, p.v)
                    for nb in range(4):
                        p.t.inc(nt_.matmul(
                            PF[0:1, nb * 512:nb * 512 + 512],
                            ones_col[hh * 64:(hh + 1) * 64],
                            ROWSB[hh * 64:(hh + 1) * 64, 0:2048][:, nb * 512:(nb + 1) * 512],
                            start=True, stop=True))
                    p.wait(nv, p.t)
                    p.v.inc(nv.reduce_max(ROWS[0:1, col0 + hh:col0 + hh + 1],
                                          PF[0:1, 0:2048], axis=AX.X))
            p.wait(nv, p.v)
            p.v.inc(nv.reduce_max(ROWS[0:1, 4:5],
                                  ROWS[0:1, 0:2].rearrange("p (a b) -> p a b", a=1),
                                  axis=AX.X))
            p.wait(nv, p.v)
            p.v.inc(nv.reduce_max(ROWS[0:1, 5:6],
                                  ROWS[0:1, 2:4].rearrange("p (a b) -> p a b", a=1),
                                  axis=AX.X))
            p.wait(ns, p.v)
            p.s.inc(ns.activation(ROWS[0:1, 6:8], ROWS[0:1, 4:6], AF.Sqrt))
            p.wait(nv, p.s)
            p.v.inc(nv.tensor_mul(ROWS[0:1, 8:9], ROWS[0:1, 6:7], ROWS[0:1, 7:8]))
            p.wait(nv, p.v)
            p.v.inc(nv.tensor_scalar_mul(ROWS[0:1, 8:9], ROWS[0:1, 8:9], -0.125))
            p.wait(nt_, p.v)
            p.wait(nt_, p.s)
            p.t.inc(nt_.matmul(PF[0:128, 5 * 512:5 * 512 + 1], ones_row_f32[:],
                               ROWS[0:1, 8:9], start=True, stop=True))
            p.wait(ns, p.t)
            p.s.inc(ns.copy(FMV[:, 70:71], PF[0:128, 5 * 512:5 * 512 + 1]))
            cb_s = p.s.n
            cbcol = FMV[:, 70:71]

            exp_ms = {0: cb_s, 1: cb_s}
            p.wait(nt_, p.s, cb_s)
            p.wait(nt_, p.v)
            for qb in range(4):
                nk = qb * 4 + 4
                def av_mm(kt):
                    pp_ = kt % 2
                    p.wait(nt_, p.s, exp_ms[pp_])
                    p.t.inc(nt_.matmul(
                        PF[0:65, 0:512],
                        vball[:, kt * H * 65 + 2 * hp * 65:kt * H * 65 + (2 * hp + 1) * 65],
                        pu2[pp_][:, 0:512], start=(kt == 0), stop=(kt == nk - 1)))
                    p.t.inc(nt_.matmul(
                        PF[0:65, 512:1024],
                        vball[:, kt * H * 65 + (2 * hp + 1) * 65:kt * H * 65 + (2 * hp + 2) * 65],
                        pu2[pp_][:, 512:1024], start=(kt == 0), stop=(kt == nk - 1)))
                for kt in range(nk):
                    pp_ = kt % 2
                    scb = 2 + 2 * pp_
                    p.wait(nt_, p.s, exp_ms[pp_])
                    p.t.inc(nt_.matmul(
                        bank(scb), kTr[0:64, kt * 128:(kt + 1) * 128],
                        qTr[0:64, qb * 512:(qb + 1) * 512],
                        start=True, stop=True, tile_position=(0, 0)))
                    p.t.inc(nt_.matmul(
                        bank(scb + 1), kTr[64:128, kt * 128:(kt + 1) * 128],
                        qTr[64:128, qb * 512:(qb + 1) * 512],
                        start=True, stop=True, tile_position=(64, 0)))
                    sc_t = p.t.n
                    if kt >= qb * 4:
                        r = kt - qb * 4
                        p.wait(nv, p.t, sc_t)
                        p.v.inc(nv.tensor_add(bank(scb), bank(scb),
                                              causal[:, r * 512:(r + 1) * 512]))
                        p.v.inc(nv.tensor_add(bank(scb + 1), bank(scb + 1),
                                              causal[:, r * 512:(r + 1) * 512]))
                        p.wait(ns, p.v)
                    p.wait(ns, p.t, sc_t)
                    exp_ms[pp_] = p.s.inc(ns.activation(
                        pu2[pp_][:], PF[:, scb * 512:(scb + 2) * 512],
                        AF.Exp, scale=0.125, bias=cbcol))
                    if kt >= 1:
                        av_mm(kt - 1)
                av_mm(nk - 1)
                av_t = p.t.n
                for hh in range(2):
                    base = hh * 64
                    p.wait(nv, p.t, av_t)
                    p.v.inc(nv.reciprocal(rsinv[:], PF[64:65, hh * 512:hh * 512 + 512]))
                    p.wait(nt_, p.v)
                    p.wait(nt_, p.s, exp_ms[0])
                    p.t.inc(nt_.matmul(bank(2), ones_row_f32[:], rsinv[:],
                                       start=True, stop=True))
                    p.wait(ns, p.t)
                    p.wait(ns, p.v)
                    exp_ms[0] = p.s.inc(ns.copy(rsbc[:], bank(2)))
                    p.wait(nv, p.s)
                    p.v.inc(nv.tensor_mul(
                        attT[base:base + 64, hp * S + qb * 512:hp * S + (qb + 1) * 512],
                        PF[0:64, hh * 512:hh * 512 + 512], rsbc[0:64, :]))
                p.wait(nt_, p.v)

        p.wait(ng, p.s)
        p.wait(ng, p.v)
        p.wait(ng, p.t)
        load_w_chunks(woW, ab["WoT"], D, p.w[0])
        p.wait(nt_, p.w[0])
        for t in range(NT):
            pb = 0 if (t % 2 == 0) else 2
            p.wait(nt_, p.v)
            for c in range(NC):
                for b_ in range(2):
                    p.t.inc(nt_.matmul(
                        bank(pb + b_),
                        attT[:, c * S + t * 128:c * S + (t + 1) * 128],
                        woW[:, c * D + b_ * 512:c * D + (b_ + 1) * 512],
                        start=(c == 0), stop=(c == NC - 1)))
            p.wait(nv, p.t)
            p.v.inc(nv.tensor_add(R[:, t * D:(t + 1) * D],
                                  R[:, t * D:(t + 1) * D], bank2(pb)))

    emit_moe("moe1", pre=True)
    if stage == "B":
        _emit_out(p, R, out)
        p.stack.close()
        return nc
    emit_attnb()
    if stage == "C":
        _emit_out(p, R, out)
        p.stack.close()
        return nc
    emit_moe("moe2")
    _emit_out(p, R, out)
    p.stack.close()
    return nc


def _emit_out(p, R, out):
    nc = p.nc
    p.wait(nc.sync, p.v)
    p.wait(nc.sync, p.s)
    p.d.inc(nc.sync.dma_start(out, R[:]), 16)
    p.wait(nc.sync, p.d)


def _chunk(w):
    w = np.asarray(w)
    cc = w.shape[0] // 128
    return np.ascontiguousarray(
        w.reshape(cc, 128, w.shape[1]).transpose(1, 0, 2).reshape(128, -1))


def _pairmaj(w):
    w = np.asarray(w).reshape(8, 128, 8, 128)
    return np.ascontiguousarray(
        w.transpose(1, 2, 0, 3).reshape(128, -1))


def _qmaj(w):
    w = np.asarray(w).reshape(8, 128, 4, 512)
    return np.ascontiguousarray(
        w.transpose(1, 2, 0, 3).reshape(128, -1))


def _core_inputs(inputs, c):
    i = inputs
    fm = lambda v: np.ascontiguousarray(np.asarray(v).reshape(NC, 128).T)
    row = lambda v: np.ascontiguousarray(np.asarray(v).reshape(1, -1))
    xt = np.asarray(i["L0_attn_a_x"])[c].reshape(NT, 128, D)
    m = {
        "x": np.ascontiguousarray(xt.transpose(1, 0, 2).reshape(128, -1)),
        "aa_Wq": _pairmaj(i["L0_attn_a_Wq"]), "aa_Wk": _pairmaj(i["L0_attn_a_Wk"]),
        "aa_Wv": _chunk(i["L0_attn_a_Wv"]), "aa_Wo": _chunk(i["L0_attn_a_Wo"]),
        "aa_W1": _qmaj(i["L0_attn_a_W1"]), "aa_W2": _chunk(i["L0_attn_a_W2"]),
        "aa_bq": row(i["L0_attn_a_bq"]), "aa_bk": row(i["L0_attn_a_bk"]),
        "aa_bv": row(i["L0_attn_a_bv"]), "aa_bo": row(i["L0_attn_a_bo"]),
        "aa_b1": row(i["L0_attn_a_b1"]), "aa_b2": row(i["L0_attn_a_b2"]),
        "aa_lnq_s_fm": fm(i["L0_attn_a_lnq_s"]), "aa_lnq_b_fm": fm(i["L0_attn_a_lnq_b"]),
        "aa_lnk_s_fm": fm(i["L0_attn_a_lnk_s"]), "aa_lnk_b_fm": fm(i["L0_attn_a_lnk_b"]),
        "aa_lnv_s_fm": fm(i["L0_attn_a_lnv_s"]), "aa_lnv_b_fm": fm(i["L0_attn_a_lnv_b"]),
        "aa_lnm_s_fm": fm(i["L0_attn_a_lnm_s"]), "aa_lnm_b_fm": fm(i["L0_attn_a_lnm_b"]),
        "ab_WqT": _pairmaj(np.asarray(i["L0_attn_b_Wq"]).T),
        "ab_WkT": _pairmaj(np.asarray(i["L0_attn_b_Wk"]).T),
        "ab_WvT": _chunk(np.asarray(i["L0_attn_b_Wv"]).T),
        "ab_WoT": _chunk(np.asarray(i["L0_attn_b_Wo"]).T),
    }
    for g in (0, 1):
        m[f"ff_r{g}"] = inputs[f"L0_ffn_router_w{g}"]
        for nm in ("wg", "wu", "wd"):
            w3 = np.asarray(inputs[f"L0_ffn_{nm}{g}"])
            ee, kk, nn2 = w3.shape
            w3 = w3.reshape(ee, kk // 128, 128, nn2).transpose(0, 2, 1, 3)
            m[f"ff_{nm}{g}"] = np.ascontiguousarray(w3.reshape(ee, 128, -1))
    return {k: np.ascontiguousarray(np.asarray(v, dtype=np.float32)) for k, v in m.items()}


def kernel(**inputs):
    nc = build("full")
    in_maps = [_core_inputs(inputs, c) for c in range(8)]
    res = run_bass_kernel_spmd(nc, in_maps, core_ids=list(range(8)))
    outs = [r["out"].reshape(128, NT, D).transpose(1, 0, 2).reshape(S, D)
            for r in res.results]
    return np.stack(outs).astype(np.float32)


def kernel_stage(stage, inputs):
    nc = build(stage)
    in_maps = [_core_inputs(inputs, c) for c in range(8)]
    res = run_bass_kernel_spmd(nc, in_maps, core_ids=list(range(8)))
    outs = [r["out"].reshape(128, NT, D).transpose(1, 0, 2).reshape(S, D)
            for r in res.results]
    return np.stack(outs)
